# version 07
# baseline (speedup 1.0000x reference)
import numpy as np
import ml_dtypes
import concourse.bass as bass
import concourse.mybir as mybir
from concourse.bass_utils import run_bass_kernel_spmd
from contextlib import ExitStack

F32 = mybir.dt.float32
BF16 = mybir.dt.bfloat16
AF = mybir.ActivationFunctionType
ALU = mybir.AluOpType
AX = mybir.AxisListType
BF = ml_dtypes.bfloat16

NCORES = 8
T = 2048
C = 1024
NLAYER = 12
LN_EPS = 1e-5
NP = 2048
SBUF_TOP = 229376


class KB:
    def __init__(self, n_layers=NLAYER, endgame=True, taps=(), upto="full"):
        self.n_layers = n_layers
        self.endgame = endgame
        self.upto = upto
        self.taps = set(taps)
        self.nc = bass.Bass("TRN2", target_bir_lowering=False, debug=False)
        self.streams = {"t": [], "v": [], "s": [], "g": [], "y": []}
        self.cnt = {}
        self.hw = {}
        self.guard = {}
        self.ov = SBUF_TOP - 114688 - 32

    def em(self, eng, method, *args, waits=(), incs=(), **kw):
        real_waits = []
        for wt in waits:
            if wt is None:
                continue
            sem, val = wt
            key = (eng, sem.name)
            if self.hw.get(key, -1) >= val:
                continue
            self.hw[key] = val
            real_waits.append((sem, val))
        for sem, amt in incs:
            self.cnt[sem.name] = self.cnt.get(sem.name, 0) + amt

        def fn(e, method=method, args=args, kw=kw, rw=tuple(real_waits), incs=tuple(incs)):
            for sem, val in rw:
                e.wait_ge(sem, val)
            inst = getattr(e, method)(*args, **kw)
            for sem, amt in incs:
                inst.then_inc(sem, amt)
        self.streams[eng].append(fn)

    def n(self, sem):
        return self.cnt.get(sem.name, 0)

    def use(self, res):
        return self.guard.get(res, [])

    def prod(self, res, sem):
        self.guard[res] = [(sem, self.n(sem))]

    def vsem(self):
        return (self.sV, self.n(self.sV))

    def join_all(self):
        sems = [self.sT, self.sV, self.sS] + list(self.dma_sems)
        vals = [(s, self.n(s)) for s in sems]
        for eng in ("t", "v", "s", "g", "y"):
            for s, v in vals:
                if v > 0:
                    self.em(eng, "wait_ge", s, v)

    def dma(self, eng, sem, out, in_, res_waits=(), **kw):
        self.em(eng, "dma_start", waits=res_waits, incs=[(sem, 16)],
                out=out, in_=in_, **kw)
        return self.n(sem)

    def build(self):
        nc = self.nc
        ov0 = self.ov

        d = {}
        d["x"] = nc.dram_tensor("x", [T, C], F32, kind="ExternalInput").ap()
        d["WA"] = nc.dram_tensor("WA", [NLAYER, 8, C, 512], F32, kind="ExternalInput").ap()
        d["bqkv"] = nc.dram_tensor("bqkv", [NLAYER, 3 * C], F32, kind="ExternalInput").ap()
        d["bproj"] = nc.dram_tensor("bproj", [NLAYER, C], F32, kind="ExternalInput").ap()
        d["phiT"] = nc.dram_tensor("phiT", [C, NP], F32, kind="ExternalInput").ap()
        d["phib"] = nc.dram_tensor("phib", [1, NP], F32, kind="ExternalInput").ap()
        d["EW"] = nc.dram_tensor("EW", [8, C, C], F32, kind="ExternalInput").ap()
        d["EB"] = nc.dram_tensor("EB", [8, C], F32, kind="ExternalInput").ap()
        d["ones"] = nc.dram_tensor("ones", [1, 512], BF16, kind="ExternalInput").ap()
        d["idbf"] = nc.dram_tensor("idbf", [128, 128], BF16, kind="ExternalInput").ap()
        d["idf32"] = nc.dram_tensor("idf32", [128, 128], F32, kind="ExternalInput").ap()
        d["o"] = nc.dram_tensor("o", [T, C], F32, kind="ExternalOutput").ap()

        xn_spill = nc.dram_tensor("xn_spill", [128, 8, T], BF16).ap()
        WAbf = nc.dram_tensor("WAbf", [NLAYER, 8, C, 512], BF16).ap()
        phibf = nc.dram_tensor("phibf_d", [C, NP], BF16).ap()
        s_in = nc.dram_tensor("s_in", [128, 512], F32).ap()
        s_out = nc.dram_tensor("s_out", [128, 512], F32).ap()
        rs_in = [nc.dram_tensor(f"rs_in{h}", [1024, C], F32).ap() for h in range(2)]
        rs_out = [nc.dram_tensor(f"rs_out{h}", [128, C], F32).ap() for h in range(2)]
        ag_in = nc.dram_tensor("ag_in", [256, C], BF16).ap()
        ag_out = nc.dram_tensor("ag_out", [NP, C], BF16, addr_space="Shared").ap()

        taps = {}
        if "xn" in self.taps:
            taps["d_xn"] = nc.dram_tensor("d_xn", [128, 8, T], F32, kind="ExternalOutput").ap()
        if "xhalf" in self.taps:
            taps["d_xh"] = nc.dram_tensor("d_xh", [128, 8, C], F32, kind="ExternalOutput").ap()
        if "rs" in self.taps:
            taps["d_rs"] = nc.dram_tensor("d_rs", [2, 128, C], F32, kind="ExternalOutput").ap()
        if "yt" in self.taps:
            taps["d_yt"] = nc.dram_tensor("d_yt", [128, 2, C], BF16, kind="ExternalOutput").ap()
        if "S" in self.taps:
            taps["d_S"] = nc.dram_tensor("d_S", [128, 512], F32, kind="ExternalOutput").ap()
        if "xa" in self.taps:
            taps["d_xa"] = nc.dram_tensor("d_xa", [128, 8, T], F32, kind="ExternalOutput").ap()

        with ExitStack() as ctx:
            ecx = ctx.enter_context

            x_attn = ecx(nc.sbuf_tensor("x_attn", [128, 8, T], F32))
            ones = ecx(nc.sbuf_tensor("ones_sb", [1, 512], BF16))
            idbf = ecx(nc.sbuf_tensor("idbf_sb", [128, 128], BF16))
            idf32 = ecx(nc.sbuf_tensor("idf32_sb", [128, 128], F32))
            phib_bf = ecx(nc.sbuf_tensor("phib_bf", [1, NP], BF16))
            S_sb = ecx(nc.sbuf_tensor("S_sb", [128, 512], F32))
            S_bd = ecx(nc.sbuf_tensor("S_bd", [128, 8, 128], BF16))
            sm = ecx(nc.sbuf_tensor("sm", [128, 64], F32))
            cst = ecx(nc.sbuf_tensor("cst", [128, 2], F32))
            zer = ecx(nc.sbuf_tensor("zer", [1, 128], BF16))

            AT = nc.alloc_sbuf_tensor_at
            o = ov0
            xin = AT("xin", [128, 2, C], F32, offset=o); o += 8192
            xf = AT("xf", [128, 2, C], F32, offset=o); o += 8192
            xsq = AT("xsq", [128, 2, C], F32, offset=o); o += 8192
            xn_tc = AT("xn_tc", [128, 16, C], BF16, offset=o); o += 32768
            xnct_st = AT("xnct_st", [128, 2, 8, 128], BF16, offset=o); o += 4096
            X_half = AT("X_half", [128, 8, C], F32, offset=o); o += 32768
            phi_sl = AT("phi_sl", [128, 2, 8, 384], BF16, offset=o); o += 12288
            E2 = AT("E2", [128, 2, 384], F32, offset=o); o += 3072
            D2 = AT("D2", [128, 4, 384], BF16, offset=o); o += 3072
            gs12 = AT("gs12", [128, 2, 12], F32, offset=o); o += 128
            assert o <= SBUF_TOP - 32, f"phase01 overlay {o}"

            o = ov0
            wring = AT("wring", [128, 4, 8, 512], BF16, offset=o); o += 32768
            staging = AT("staging", [128, 2, 8, 512], BF16, offset=o); o += 16384
            kv_buf = AT("kv_buf", [128, 2, 4, 1024], BF16, offset=o); o += 16384
            q_buf = AT("q_buf", [128, 8, 512], BF16, offset=o); o += 8192
            y_buf = AT("y_buf", [128, 8, 512], BF16, offset=o); o += 8192
            bkv_bf = AT("bkv_bf", [1, 2048], BF16, offset=o); o += 4096
            bqp_bf = AT("bqp_bf", [1, 2048], BF16, offset=o); o += 4096
            EWq = AT("EWq", [128, 2, 8, 256], BF16, offset=o); o += 8192
            xt = AT("xt", [128, C], F32, offset=o); o += 4096
            xtb = AT("xtb", [128, C], BF16, offset=o); o += 2048
            Xh = AT("Xh", [128, 2, 8, 128], BF16, offset=o); o += 4096
            yt_sl = AT("yt_sl", [128, 2, C], BF16, offset=o); o += 4096
            eb_bf = AT("eb_bf", [1, 1, C], BF16, offset=o); o += 2048
            assert o <= SBUF_TOP - 32, f"attn overlay {o}"
            exsq = kv_buf.ap()[:, 0, 0:2, :].bitcast(F32).rearrange("q a b -> q (a b)")

            o = ov0
            phi_full = AT("phi_full", [128, 8, NP], BF16, offset=o); o += 32768
            Ct = AT("Ct", [128, 16, 512], BF16, offset=o); o += 16384
            Ebuf = AT("Ebuf", [128, NP], F32, offset=o); o += 8192
            Cbuf = AT("Cbuf", [128, 3, NP], BF16, offset=o); o += 12288
            th_s = AT("th_s", [128, 512], F32, offset=o); o += 2048
            gs_buf = AT("gs_buf", [128, 512], F32, offset=o); o += 2048
            sgx = AT("sgx", [128, 8, 512], F32, offset=o); o += 16384
            outt = AT("outt", [128, 2, C], F32, offset=o); o += 8192
            xo = AT("xo", [128, 2, C], F32, offset=o); o += 8192
            yt2 = AT("yt2", [128, 2, C], BF16, offset=o); o += 4096
            assert o <= SBUF_TOP - 32, f"endgame overlay {o}"
            xntau = AT("xntau", [128, 2, 8, 128], BF16, offset=ov0 - 4096)

            pb = [ecx(nc.psum_tensor(f"pb{i}", [128, 512], F32)) for i in range(8)]

            self.sT = ecx(nc.semaphore("sT"))
            self.sV = ecx(nc.semaphore("sV"))
            self.sS = ecx(nc.semaphore("sS"))
            ccS = [ecx(nc.semaphore(f"ccS{i}")) for i in range(self.n_layers)]
            ccRS = [ecx(nc.semaphore("ccRS0")), ecx(nc.semaphore("ccRS1"))]
            ccAG = ecx(nc.semaphore("ccAG"))
            sT, sV, sS = self.sT, self.sV, self.sS
            dCst = ecx(nc.semaphore("dCst"))
            dXi = [ecx(nc.semaphore(f"dXi{p}")) for p in range(2)]
            dSp = [ecx(nc.semaphore(f"dSp{p}")) for p in range(2)]
            dXn = [ecx(nc.semaphore(f"dXn{p}")) for p in range(2)]
            dPh = [ecx(nc.semaphore(f"dPh{p}")) for p in range(2)]
            dXh = ecx(nc.semaphore("dXh"))
            dW = [ecx(nc.semaphore(f"dW{p}")) for p in range(4)]
            dBk = ecx(nc.semaphore("dBk"))
            dBq = ecx(nc.semaphore("dBq"))
            dXt = ecx(nc.semaphore("dXt"))
            dEb = ecx(nc.semaphore("dEb"))
            dEw = [ecx(nc.semaphore(f"dEw{p}")) for p in range(2)]
            dAg = ecx(nc.semaphore("dAg"))
            dSio = ecx(nc.semaphore("dSio"))
            dPf = ecx(nc.semaphore("dPf"))
            dYt = [ecx(nc.semaphore(f"dYt{p}")) for p in range(2)]
            dOu = [ecx(nc.semaphore(f"dOu{p}")) for p in range(2)]
            dTap = ecx(nc.semaphore("dTap"))
            dWC0 = ecx(nc.semaphore("dWC0"))
            dWC1 = ecx(nc.semaphore("dWC1"))
            dPc = ecx(nc.semaphore("dPc"))
            self.dma_sems = ([dCst, dXh, dBk, dBq, dXt, dEb, dAg, dSio,
                              dPf, dTap, dWC0, dWC1, dPc]
                             + dSp + dXi + dXn + dPh + dW + dEw + dYt + dOu)

            block = ecx(nc.Block())

            self.dma("g", dCst, ones.ap(), d["ones"])
            self.dma("g", dCst, idbf.ap(), d["idbf"])
            self.dma("g", dCst, idf32.ap(), d["idf32"])
            self.dma("g", dCst, phib_bf.ap(), d["phib"])
            consts_dn = (dCst, self.n(dCst))
            self.em("v", "memset", cst.ap()[:, 0:1], float(LN_EPS))
            self.em("v", "memset", zer.ap()[:, :], 0.0)
            self.em("v", "memset", cst.ap()[:, 1:2], 0.5, incs=[(sV, 1)])
            cst_dn = (sV, self.n(sV))

            one_c = 1.0 / C

            wc_chunks = [(i, sl) for i in range(min(self.n_layers, NLAYER))
                         for sl in range(8)]
            wc_pos = [0]

            def emit_wcast(n):
                for _ in range(n):
                    if wc_pos[0] >= len(wc_chunks):
                        return
                    i, sl = wc_chunks[wc_pos[0]]
                    sem = dWC0 if wc_pos[0] < 12 else dWC1
                    self.dma("g", sem, WAbf[i, sl], d["WA"][i, sl])
                    wc_pos[0] += 1

            emit_wcast(12)
            self.dma("g", dPc, phibf, d["phiT"])
            w_phicast = (dPc, self.n(dPc))

            def ln_stats(src_ap, w_src, sq_ap, sq_guard_key):
                self.em("v", "tensor_mul", sq_ap, src_ap, src_ap,
                        waits=[w_src, *self.use(sq_guard_key), *self.use("sm")],
                        incs=[(sV, 1)])
                w_mul = self.vsem()
                self.em("v", "reduce_sum", sm.ap()[:, 0:1], src_ap, axis=AX.X,
                        negate=True, incs=[(sV, 1)])
                self.em("v", "reduce_sum", sm.ap()[:, 1:2], sq_ap, axis=AX.X,
                        waits=[w_mul], incs=[(sV, 1)])
                self.prod(sq_guard_key, sV)
                self.em("v", "tensor_scalar_mul", sm.ap()[:, 2:3], sm.ap()[:, 0:1],
                        one_c, waits=[self.vsem()], incs=[(sV, 1)])
                self.em("v", "tensor_scalar_mul", sm.ap()[:, 3:4], sm.ap()[:, 1:2],
                        one_c, incs=[(sV, 1)])
                self.em("v", "tensor_mul", sm.ap()[:, 4:5], sm.ap()[:, 2:3],
                        sm.ap()[:, 2:3], waits=[self.vsem()], incs=[(sV, 1)])
                self.em("v", "tensor_sub", sm.ap()[:, 5:6], sm.ap()[:, 3:4],
                        sm.ap()[:, 4:5], waits=[self.vsem()], incs=[(sV, 1)])
                self.em("s", "activation", sm.ap()[:, 6:7], sm.ap()[:, 5:6], AF.Sqrt,
                        bias=cst.ap()[:, 0:1], waits=[self.vsem(), cst_dn],
                        incs=[(sS, 1)])
                self.em("v", "reciprocal", sm.ap()[:, 7:8], sm.ap()[:, 6:7],
                        waits=[(sS, self.n(sS))], incs=[(sV, 1)])
                self.em("v", "tensor_mul", sm.ap()[:, 8:9], sm.ap()[:, 2:3],
                        sm.ap()[:, 7:8], waits=[self.vsem()], incs=[(sV, 1)])
                return (sV, self.n(sV))

            for tt in range(16):
                par = tt % 2
                ld = self.dma("y", dXi[par], xin.ap()[:, par],
                              d["x"][tt * 128:(tt + 1) * 128, :],
                              res_waits=self.use(f"xin{par}"))
                xi = xin.ap()[:, par]
                self.em("v", "tensor_mul", xsq.ap()[:, par], xi, xi,
                        waits=[(dXi[par], ld), *self.use(f"xsq{par}")],
                        incs=[(sV, 1)])
                w_mul = self.vsem()
                self.em("v", "reduce_sum", sm.ap()[:, tt:tt + 1], xi, axis=AX.X,
                        negate=True, incs=[(sV, 1)])
                self.guard[f"xin{par}"] = [(sV, self.n(sV))]
                self.em("v", "reduce_sum", sm.ap()[:, 16 + tt:17 + tt],
                        xsq.ap()[:, par], axis=AX.X, waits=[w_mul], incs=[(sV, 1)])
                self.prod(f"xsq{par}", sV)
            self.em("v", "tensor_scalar_mul", sm.ap()[:, 0:16], sm.ap()[:, 0:16],
                    one_c, waits=[self.vsem()], incs=[(sV, 1)])
            self.em("v", "tensor_scalar_mul", sm.ap()[:, 16:32], sm.ap()[:, 16:32],
                    one_c, incs=[(sV, 1)])
            self.em("v", "tensor_mul", sm.ap()[:, 32:48], sm.ap()[:, 0:16],
                    sm.ap()[:, 0:16], waits=[self.vsem()], incs=[(sV, 1)])
            self.em("v", "tensor_sub", sm.ap()[:, 16:32], sm.ap()[:, 16:32],
                    sm.ap()[:, 32:48], waits=[self.vsem()], incs=[(sV, 1)])
            self.em("s", "activation", sm.ap()[:, 32:48], sm.ap()[:, 16:32], AF.Sqrt,
                    bias=cst.ap()[:, 0:1], waits=[self.vsem(), cst_dn], incs=[(sS, 1)])
            self.em("v", "reciprocal", sm.ap()[:, 48:64], sm.ap()[:, 32:48],
                    waits=[(sS, self.n(sS))], incs=[(sV, 1)])
            self.em("v", "tensor_mul", sm.ap()[:, 32:48], sm.ap()[:, 0:16],
                    sm.ap()[:, 48:64], waits=[self.vsem()], incs=[(sV, 1)])
            w_stats = (sV, self.n(sV))
            w_stats_s = (sS, self.n(sS))
            for tt in range(16):
                par = tt % 2
                ld = self.dma("y", dXi[par], xin.ap()[:, par],
                              d["x"][tt * 128:(tt + 1) * 128, :],
                              res_waits=self.use(f"xin{par}"))
                xi = xin.ap()[:, par]
                w_ld = (dXi[par], ld)
                self.em("s", "activation", xn_tc.ap()[:, tt, :], xi, AF.Identity,
                        bias=sm.ap()[:, 32 + tt:33 + tt],
                        scale=sm.ap()[:, 48 + tt:49 + tt],
                        waits=[w_ld, w_stats], incs=[(sS, 1)])
                w_xnb = (sS, self.n(sS))
                self.em("v", "tensor_scalar", xf.ap()[:, par], xi,
                        sm.ap()[:, tt:tt + 1], sm.ap()[:, 48 + tt:49 + tt],
                        op0=ALU.add, op1=ALU.mult,
                        waits=[w_ld, w_stats, *self.use(f"xf{par}")], incs=[(sV, 1)])
                w_xnf = (sV, self.n(sV))
                self.guard[f"xin{par}"] = [(sV, self.n(sV)), (sS, self.n(sS))]
                for j in range(8):
                    bfv = pb[j].ap()[:, :].bitcast(BF16)
                    self.em("t", "transpose", bfv[:, 0:128],
                            xn_tc.ap()[:, tt, j * 128:(j + 1) * 128], idbf.ap(),
                            waits=[w_xnb, consts_dn, *self.use(f"pb{j}")],
                            incs=([(sT, 1)] if j == 7 else ()))
                w_trb = (sT, self.n(sT))
                spar = tt % 2
                for j in range(8):
                    bfv = pb[j].ap()[:, :].bitcast(BF16)
                    self.em("v", "tensor_copy", xnct_st.ap()[:, spar, j, :],
                            bfv[:, 0:128],
                            waits=([w_trb, *self.use(f"xnct{spar}")] if j == 0 else ()),
                            incs=([(sV, 1)] if j == 7 else ()))
                w_ctev = (sV, self.n(sV))
                for j in range(8):
                    self.em("t", "transpose", pb[j].ap()[:, 0:128],
                            xf.ap()[:, par, j * 128:(j + 1) * 128], idf32.ap(),
                            waits=[w_xnf, w_ctev],
                            incs=([(sT, 1)] if j == 7 else ()))
                w_trf = (sT, self.n(sT))
                self.prod(f"xf{par}", sT)
                for j in range(8):
                    self.em("v", "tensor_copy",
                            x_attn.ap()[:, j, tt * 128:(tt + 1) * 128],
                            pb[j].ap()[:, 0:128],
                            waits=([w_trf] if j == 0 else ()),
                            incs=([(sV, 1)] if j == 7 else ()))
                for j in range(8):
                    self.prod(f"pb{j}", sV)
                sp = self.dma("y", dSp[spar], xn_spill[:, :, tt * 128:(tt + 1) * 128],
                              xnct_st.ap()[:, spar], res_waits=[w_ctev])
                self.guard[f"xnct{spar}"] = [(dSp[spar], sp)]

            if "xn" in self.taps:
                self.em("v", "engine_nop", incs=[(sV, 1)])
                self.dma("y", dTap, taps["d_xn"][:, :, :], x_attn.ap()[:, :, :],
                         res_waits=[(sV, self.n(sV))])
            if self.upto == "p0":
                self.em("v", "engine_nop", incs=[(sV, 1)])
                self.dma("y", dOu[0], d["o"][0:128, :], x_attn.ap()[:, 0, 0:1024],
                         res_waits=[(sV, self.n(sV))])
                self.em("y", "wait_ge", dOu[0], self.n(dOu[0]))
                self.em("y", "wait_ge", dTap, self.n(dTap)) if self.n(dTap) else None
                self.finish(block)
                return d, taps

            groups = [(0, 2), (2, 2), (4, 2), (6, 2)]
            lgbank = [0, 1, 6, 7]
            phase0_done = [(sV, self.n(sV)), (sS, self.n(sS)), (sT, self.n(sT))]
            gi_all = 0
            for H in range(2):
                for (b0g, gsz) in groups:
                    gpar = gi_all % 2
                    cols = gsz * 128
                    e0 = (8 * H + b0g) * 128
                    ng = cols // 32
                    pld = self.dma(
                        "g", dPh[gpar], phi_sl.ap()[:, gpar, :, 0:cols],
                        d["phiT"].rearrange("(cc p) e -> p cc e", p=128)[:, :, e0:e0 + cols],
                        res_waits=self.use(f"phi{gpar}"))
                    w_phi = (dPh[gpar], pld)
                    emit_wcast(6 if gi_all > 0 else 0)
                    if gi_all < 3 and self.n_layers > 0:
                        r = gi_all
                        ld0 = self.dma("y", dW[r], wring.ap()[:, r],
                                       WAbf[0, r].rearrange("(cc p) f -> p cc f", p=128),
                                       res_waits=[*phase0_done, (dWC0, 192)])
                        self.guard[f"ring{r}_ld"] = [(dW[r], ld0)]


                    def emit_xacc(tt, w_D):
                        for j in range(gsz):
                            for hd in range(2):
                                bank = 2 + j * 2 + hd
                                last = (j == gsz - 1 and hd == 1)
                                self.em("t", "matmul", pb[bank].ap()[:, :],
                                        D2.ap()[:, tt % 4, j * 128:(j + 1) * 128],
                                        xn_tc.ap()[:, tt, hd * 512:(hd + 1) * 512],
                                        start=(tt == 0), stop=(tt == 15),
                                        waits=([w_D, *self.use(f"pb{bank}")]
                                               if tt == 0 else [w_D]),
                                        incs=([(sT, 1)] if (last or tt == 15) else ()))
                        self.guard[f"D2{tt % 4}"] = [(sT, self.n(sT))]

                    wds = {}
                    for tt in range(16):
                        tpar = tt % 2
                        lb = lgbank[tt % 4]
                        xld = self.dma("y", dXn[tpar], xnct_st.ap()[:, tpar],
                                       xn_spill[:, :, tt * 128:(tt + 1) * 128],
                                       res_waits=[(dSp[0], self.n(dSp[0])),
                                                  (dSp[1], self.n(dSp[1])),
                                                  *self.use(f"xnct{tpar}")])
                        w_xn = (dXn[tpar], xld)
                        pl = pb[lb]
                        for cc in range(8):
                            self.em("t", "matmul", pl.ap()[:, 0:cols],
                                    xnct_st.ap()[:, tpar, cc, :],
                                    phi_sl.ap()[:, gpar, cc, 0:cols],
                                    start=(cc == 0), stop=False,
                                    waits=([w_phi, w_xn, *phase0_done,
                                            *self.use(f"pb{lb}")] if cc == 0 else ()))
                        self.em("t", "matmul", pl.ap()[:, 0:cols],
                                ones.ap()[0:1, 0:128], phib_bf.ap()[0:1, e0:e0 + cols],
                                start=False, stop=True, incs=[(sT, 1)])
                        self.guard[f"xnct{tpar}"] = [(sT, self.n(sT))]
                        w_log = (sT, self.n(sT))
                        if tt >= 2:
                            emit_xacc(tt - 2, wds.pop(tt - 2))
                        self.em("v", "reduce_max", sm.ap()[:, 20:21], pl.ap()[:, 0:cols],
                                axis=AX.X, negate=True,
                                waits=[w_log, *self.use("smE")], incs=[(sV, 1)])
                        self.em("s", "activation", E2.ap()[:, tpar, 0:cols],
                                pl.ap()[:, 0:cols],
                                AF.Exp, bias=sm.ap()[:, 20:21],
                                waits=[(sV, self.n(sV)), *self.use(f"E2{tpar}")],
                                incs=[(sS, 1)])
                        self.prod(f"pb{lb}", sS)
                        self.prod("smE", sS)
                        w_exp = (sS, self.n(sS))
                        self.em("v", "reduce_sum",
                                gs12.ap()[:, tpar, 0:ng, None],
                                E2.ap()[:, tpar, 0:cols].rearrange("q (n p) -> q n p", p=32),
                                axis=AX.X, waits=[w_exp], incs=[(sV, 1)])
                        self.em("v", "reciprocal", gs12.ap()[:, tpar, 0:ng],
                                gs12.ap()[:, tpar, 0:ng],
                                waits=[self.vsem()], incs=[(sV, 1)])
                        self.em("v", "tensor_tensor",
                                D2.ap()[:, tt % 4, 0:cols].rearrange("q (n p) -> q n p", p=32),
                                E2.ap()[:, tpar, 0:cols].rearrange("q (n p) -> q n p", p=32),
                                gs12.ap()[:, tpar, 0:ng, None].broadcast_to([128, ng, 32]),
                                op=ALU.mult,
                                waits=[self.vsem(), *self.use(f"D2{tt % 4}")],
                                incs=[(sV, 1)])
                        self.prod(f"E2{tpar}", sV)
                        wds[tt] = (sV, self.n(sV))
                    emit_xacc(14, wds.pop(14))
                    emit_xacc(15, wds.pop(15))
                    w_acc = (sT, self.n(sT))
                    for j in range(gsz):
                        for hd in range(2):
                            bank = 2 + j * 2 + hd
                            self.em("v", "tensor_copy",
                                    X_half.ap()[:, b0g + j, hd * 512:(hd + 1) * 512],
                                    pb[bank].ap()[:, :],
                                    waits=[w_acc, *self.use("XhW")], incs=[(sV, 1)])
                            self.prod(f"pb{bank}", sV)
                    self.prod(f"phi{gpar}", sT)
                    gi_all += 1
                w_ev = (sV, self.n(sV))
                if H == 0 and "xhalf" in self.taps:
                    self.dma("y", dTap, taps["d_xh"][:, :, :], X_half.ap()[:, :, :],
                             res_waits=[w_ev])
                xd = self.dma("g", dXh, rs_in[H].rearrange("(blk p) c -> p blk c", p=128),
                              X_half.ap()[:, :, :], res_waits=[w_ev])
                self.guard["XhW"] = [(dXh, xd)]
                if self.upto != "p1nors":
                    self.em("g", "collective_compute", "ReduceScatter", ALU.add,
                            replica_groups=[list(range(8))],
                            ins=[rs_in[H]], outs=[rs_out[H]],
                            waits=[(dXh, xd)], incs=[(ccRS[H], 1)])

            if "rs" in self.taps:
                for H in range(2):
                    self.em("y", "wait_ge", ccRS[H], 1)
                    self.dma("y", dTap, taps["d_rs"][H], rs_out[H])

            if self.upto in ("p1", "p1nors"):
                if self.upto == "p1":
                    self.em("g", "wait_ge", ccRS[0], 1)
                    self.em("g", "wait_ge", ccRS[1], 1)
                self.em("v", "engine_nop", incs=[(sV, 1)])
                self.dma("y", dOu[0], d["o"][0:128, :], x_attn.ap()[:, 0, 0:1024],
                         res_waits=[(sV, self.n(sV))])
                self.em("y", "wait_ge", dOu[0], self.n(dOu[0]))
                if self.n(dTap):
                    self.em("y", "wait_ge", dTap, self.n(dTap))
                self.finish(block)
                return d, taps

            self.join_all()
            self.em("v", "memset", S_bd.ap()[:, :, :], 0.0, incs=[(sV, 1)])

            def load_slot(i, sl):
                r = (i * 8 + sl) % 4
                wcw = ((dWC0, 192) if i * 8 + sl < 12
                       else (dWC1, self.n(dWC1)))
                ld = self.dma("y", dW[r], wring.ap()[:, r],
                              WAbf[i, sl].rearrange("(cc p) f -> p cc f", p=128),
                              res_waits=[wcw, *self.use(f"ring{r}")])
                self.guard[f"ring{r}_ld"] = [(dW[r], ld)]

            def load_bkv(i):
                ld = self.dma("g", dBk, bkv_bf.ap()[0:1, :],
                              d["bqkv"][i:i + 1, 1024:3072],
                              res_waits=self.use("bkvU"))
                self.guard["bkv_ld"] = [(dBk, ld)]

            def load_bqp(i):
                self.dma("g", dBq, bqp_bf.ap()[0:1, 0:1024], d["bqkv"][i:i + 1, 0:1024],
                         res_waits=self.use("bqpU"))
                l2 = self.dma("g", dBq, bqp_bf.ap()[0:1, 1024:2048], d["bproj"][i:i + 1, :])
                self.guard["bqp_ld"] = [(dBq, l2)]

            emit_wcast(len(wc_chunks))
            if self.n_layers > 0:
                if "ring0_ld" not in self.guard:
                    load_slot(0, 0); load_slot(0, 1); load_slot(0, 2)
                load_slot(0, 3)
                load_bkv(0); load_bqp(0)

            exp_step = [0]

            def emit_expert_chunk(step):
                if step < 4:
                    s = step // 2
                    part = step % 2
                    if part == 0:
                        ld = self.dma("g", dXt, xt.ap()[:, :], rs_out[s],
                                      res_waits=[(ccRS[s], 1), *self.use("xt")])
                        self.em("v", "tensor_scalar", xt.ap()[:, :], xt.ap()[:, :],
                                -33000.0, 65000.0, op0=ALU.max, op1=ALU.min,
                                waits=[(dXt, ld)], incs=[(sV, 1)])
                        w_clip = (sV, self.n(sV))
                        w_st = ln_stats(xt.ap()[:, :], w_clip, exsq, "exsq")
                        self.em("s", "activation", xtb.ap()[:, :], xt.ap()[:, :], AF.Tanh,
                                bias=sm.ap()[:, 8:9], scale=sm.ap()[:, 7:8],
                                waits=[w_st], incs=[(sS, 1)])
                        self.prod("sm", sS)
                        self.prod("xt", sS)
                        self.guard["extanh"] = [(sS, self.n(sS))]
                        return
                    w_tanh = self.guard["extanh"][0]
                    for r in range(4):
                        for b in range(2):
                            j = 2 * r + b
                            bank = 6 + b
                            bfv = pb[bank].ap()[:, :].bitcast(BF16)
                            self.em("t", "transpose", bfv[:, 0:128],
                                    xtb.ap()[:, j * 128:(j + 1) * 128], idbf.ap(),
                                    waits=[w_tanh, *self.use(f"pb{bank}")],
                                    incs=[(sT, 1)])
                            self.em("v", "tensor_copy", Xh.ap()[:, s, j, :],
                                    bfv[:, 0:128],
                                    waits=[(sT, self.n(sT))], incs=[(sV, 1)])
                            self.prod(f"pb{bank}", sV)
                    for tl in range(4):
                        for fc in range(2):
                            self.guard[f"kv0_{tl}_{fc}"] = [(sV, self.n(sV))]
                    self.prod(f"XhOK{s}", sV)
                else:
                    j = step - 4
                    s, el = j // 4, j % 4
                    bld = self.dma("g", dEb, eb_bf.ap()[0], d["EB"][j:j + 1, :],
                                   res_waits=self.use("eb"))
                    for qq in range(4):
                        qpar = (j * 4 + qq) % 2
                        wld = self.dma(
                            "g", dEw[qpar], EWq.ap()[:, qpar],
                            d["EW"][j].rearrange("(cc p) f -> p cc f", p=128)[:, :, qq * 256:(qq + 1) * 256],
                            res_waits=self.use(f"EWq{qpar}"))
                        bank = 6 + (j * 4 + qq) % 2
                        for cc in range(8):
                            self.em("t", "matmul", pb[bank].ap()[0:32, 0:256],
                                    Xh.ap()[:, s, cc, el * 32:(el + 1) * 32],
                                    EWq.ap()[:, qpar, cc, :],
                                    start=(cc == 0), stop=False,
                                    waits=([(dEw[qpar], wld), *self.use(f"XhOK{s}"),
                                            *self.use(f"pb{bank}")] if cc == 0 else ()))
                        self.em("t", "matmul", pb[bank].ap()[0:32, 0:256],
                                ones.ap()[0:1, 0:32],
                                eb_bf.ap()[0, :, qq * 256:(qq + 1) * 256],
                                start=False, stop=True,
                                waits=[(dEb, bld)], incs=[(sT, 1)])
                        self.guard[f"EWq{qpar}"] = [(sT, self.n(sT))]
                        self.guard["eb"] = [(sT, self.n(sT))]
                        self.em("v", "tensor_copy",
                                yt_sl.ap()[el * 32:(el + 1) * 32, s, qq * 256:(qq + 1) * 256],
                                pb[bank].ap()[0:32, 0:256],
                                waits=[(sT, self.n(sT))], incs=[(sV, 1)])
                        self.prod(f"pb{bank}", sV)
                if step == 11:
                    w_yt = (sV, self.n(sV))
                    if "yt" in self.taps:
                        self.dma("y", dTap, taps["d_yt"][:, :, :], yt_sl.ap()[:, :, :],
                                 res_waits=[w_yt])
                    ad = self.dma("g", dAg, ag_in.rearrange("(s p) c -> p s c", p=128),
                                  yt_sl.ap()[:, :, :], res_waits=[w_yt])
                    self.em("g", "collective_compute", "AllGather", ALU.bypass,
                            replica_groups=[list(range(8))],
                            ins=[ag_in], outs=[ag_out],
                            waits=[(dAg, ad)], incs=[(ccAG, 1)])

            def inject_expert():
                if exp_step[0] < 12:
                    emit_expert_chunk(exp_step[0])
                    exp_step[0] += 1

            for i in range(self.n_layers):
                self.em("t", "matmul", pb[6].ap()[:, :],
                        zer.ap()[0:1, :], ones.ap()[0:1, 0:512],
                        start=True, stop=False,
                        waits=[cst_dn, consts_dn, *self.use("pb6")])
                for H in range(2):
                    kslot = (i * 8 + 2 * H) % 4
                    vslot = (i * 8 + 2 * H + 1) % 4
                    if H == 0:
                        if i > 0:
                            load_slot(i, 2); load_slot(i, 3)
                    else:
                        load_slot(i, 4); load_slot(i, 5)
                    brep = y_buf.ap()[:, 0:4, :].bitcast(F32).rearrange("q a b -> q (a b)")
                    for fc in range(2):
                        boff = 1024 * fc + 512 * H
                        self.em("t", "matmul", pb[4 + fc].ap()[:, :],
                                ones.ap()[0:1, 0:128],
                                bkv_bf.ap()[0:1, boff:boff + 512],
                                start=True, stop=True,
                                waits=[consts_dn, *self.use("bkv_ld"),
                                       *self.use(f"pb{4 + fc}")],
                                incs=[(sT, 1)])
                    w_brepmm = (sT, self.n(sT))
                    for fc in range(2):
                        self.em("v", "tensor_copy", brep[:, fc * 512:(fc + 1) * 512],
                                pb[4 + fc].ap()[:, :],
                                waits=[w_brepmm,
                                       *(self.use("brepW") if fc == 0 else ())],
                                incs=[(sV, 1)])
                        self.prod(f"pb{4 + fc}", sV)
                    w_brep = (sV, self.n(sV))

                    def emit_S(tl, spar, H=H):
                        w_kv = self.use(f"kvE{spar}_{tl}")
                        for hl in range(8):
                            h = 8 * H + hl
                            self.em("t", "matmul",
                                    pb[6].ap()[64 * (h % 2):64 * (h % 2) + 64,
                                               64 * (h // 2):64 * (h // 2) + 64],
                                    kv_buf.ap()[:, spar, tl, 64 * hl:64 * hl + 64],
                                    kv_buf.ap()[:, spar, tl, 512 + 64 * hl:512 + 64 * hl + 64],
                                    start=False, stop=False,
                                    waits=(w_kv if hl == 0 else ()),
                                    incs=([(sT, 1)] if hl == 7 else ()))
                        self.guard[f"kv{spar}_{tl}_0"] = [(sT, self.n(sT))]
                        self.guard[f"kv{spar}_{tl}_1"] = [(sT, self.n(sT))]

                    for tau in range(4):
                        spar = tau % 2
                        self.em("s", "copy",
                                staging.ap()[:, spar],
                                x_attn.ap()[:, :, tau * 512:(tau + 1) * 512],
                                waits=[*self.use(f"stg{spar}"), *self.use(f"xa{tau}")],
                                incs=[(sS, 1)])
                        w_stg = (sS, self.n(sS))
                        for tl in range(4):
                            for fc in range(2):
                                bank = (2 * tl + fc) % 4
                                slot = kslot if fc == 0 else vslot
                                for cc in range(8):
                                    self.em("t", "matmul", pb[bank].ap()[:, :],
                                            staging.ap()[:, spar, cc, tl * 128:(tl + 1) * 128],
                                            wring.ap()[:, slot, cc, :],
                                            start=(cc == 0), stop=(cc == 7),
                                            waits=([w_stg,
                                                    *self.use(f"ring{kslot}_ld"),
                                                    *self.use(f"ring{vslot}_ld"),
                                                    *self.use(f"pb{bank}")] if cc == 0 else ()),
                                            incs=([(sT, 1)] if cc == 7 else ()))
                                self.em("v", "tensor_tensor",
                                        kv_buf.ap()[:, spar, tl, fc * 512:(fc + 1) * 512],
                                        pb[bank].ap()[:, :],
                                        brep[:, fc * 512:(fc + 1) * 512],
                                        op=ALU.add,
                                        waits=[(sT, self.n(sT)), w_brep,
                                               *self.use(f"kv{spar}_{tl}_{fc}")],
                                        incs=[(sV, 1)])
                                self.prod(f"pb{bank}", sV)
                            self.guard[f"kvE{spar}_{tl}"] = [(sV, self.n(sV))]
                            if tl >= 1:
                                emit_S(tl - 1, spar)
                        emit_S(3, spar)
                        self.guard[f"stg{spar}"] = [(sT, self.n(sT))]
                    self.guard[f"ring{kslot}"] = [(sT, self.n(sT))]
                    self.guard[f"ring{vslot}"] = [(sT, self.n(sT))]
                    self.guard["brepW"] = [(sV, self.n(sV))]
                self.guard["bkvU"] = [(sT, self.n(sT))]
                if i + 1 < self.n_layers:
                    load_bkv(i + 1)
                self.em("t", "matmul", pb[6].ap()[:, :],
                        zer.ap()[0:1, :], ones.ap()[0:1, 0:512],
                        start=False, stop=True, incs=[(sT, 1)])
                w_Sdone = (sT, self.n(sT))
                self.em("v", "tensor_copy", S_sb.ap()[:, :], pb[6].ap()[:, :],
                        waits=[w_Sdone, *self.use("Ssb")], incs=[(sV, 1)])
                self.prod("pb6", sV)
                sd1 = self.dma("g", dSio, s_in, S_sb.ap()[:, :],
                               res_waits=[(sV, self.n(sV))])
                self.em("g", "collective_compute", "AllReduce", ALU.add,
                        replica_groups=[[0, 1, 2, 3], [4, 5, 6, 7]],
                        ins=[s_in], outs=[s_out],
                        waits=[(dSio, sd1)], incs=[(ccS[i], 1)])
                sd2 = self.dma("g", dSio, S_sb.ap()[:, :], s_out,
                               res_waits=[(ccS[i], 1)])
                w_Sback = (dSio, sd2)
                for hp in range(8):
                    for par in range(2):
                        self.em("v", "tensor_scalar_mul",
                                S_bd.ap()[64 * par:64 * par + 64, hp, 64 * par:64 * par + 64],
                                S_sb.ap()[64 * par:64 * par + 64, 64 * hp:64 * hp + 64],
                                0.125,
                                waits=([w_Sback, *self.use("Sbd")]
                                       if (hp == 0 and par == 0) else ()),
                                incs=([(sV, 1)] if (hp == 7 and par == 1) else ()))
                self.prod("Ssb", sV)
                w_Sbd = (sV, self.n(sV))

                load_slot(i, 6); load_slot(i, 7)
                q0 = (i * 8 + 4) % 4
                q1 = (i * 8 + 5) % 4
                p0 = (i * 8 + 6) % 4
                p1 = (i * 8 + 7) % 4
                for j in range(16):
                    bcoff = 128 * j if j < 8 else 1024 + 128 * (j - 8)
                    self.em("t", "matmul", pb[7].ap()[:, j:j + 1],
                            bqp_bf.ap()[0:1, bcoff:bcoff + 128],
                            ones.ap()[0:1, 0:1],
                            start=True, stop=True,
                            waits=([consts_dn, *self.use("bqp_ld"), *self.use("pb7")]
                                   if j == 0 else ()),
                            incs=([(sT, 1)] if j == 15 else ()))
                self.em("v", "tensor_copy", sm.ap()[:, 32:48], pb[7].ap()[:, 0:16],
                        waits=[(sT, self.n(sT)), *self.use("bcol")], incs=[(sV, 1)])
                self.prod("pb7", sV)
                w_bcol = (sV, self.n(sV))
                for tau in range(4):
                    spar = tau % 2
                    self.em("s", "copy",
                            staging.ap()[:, spar],
                            x_attn.ap()[:, :, tau * 512:(tau + 1) * 512],
                            waits=self.use(f"stg{spar}"), incs=[(sS, 1)])
                    w_stg = (sS, self.n(sS))
                    for fq in range(8):
                        bank = fq % 2
                        slot = q0 if fq < 4 else q1
                        for cc in range(8):
                            self.em("t", "matmul", pb[bank].ap()[:, :],
                                    wring.ap()[:, slot, cc, (fq % 4) * 128:(fq % 4 + 1) * 128],
                                    staging.ap()[:, spar, cc, :],
                                    start=(cc == 0), stop=(cc == 7),
                                    waits=([w_stg, *self.use(f"ring{q0}_ld"),
                                            *self.use(f"ring{q1}_ld"),
                                            *self.use(f"pb{bank}")] if cc == 0 else ()),
                                    incs=([(sT, 1)] if cc == 7 else ()))
                        self.em("s", "activation", q_buf.ap()[:, fq, :],
                                pb[bank].ap()[:, :], AF.Identity,
                                bias=sm.ap()[:, 32 + fq:33 + fq],
                                waits=[(sT, self.n(sT)), w_bcol, *self.use(f"qb{fq}")],
                                incs=[(sS, 1)])
                        self.prod(f"pb{bank}", sS)
                    w_q = (sS, self.n(sS))
                    self.guard[f"stg{spar}"] = [(sT, self.n(sT))]
                    if tau == 3:
                        self.guard[f"ring{q0}"] = [(sT, self.n(sT))]
                        self.guard[f"ring{q1}"] = [(sT, self.n(sT))]
                        if i + 1 < self.n_layers:
                            load_slot(i + 1, 0); load_slot(i + 1, 1)
                    for hp in range(8):
                        bank = 2 + hp % 2
                        self.em("t", "matmul", pb[bank].ap()[:, :],
                                S_bd.ap()[:, hp, :], q_buf.ap()[:, hp, :],
                                start=True, stop=True,
                                waits=[w_Sbd, w_q, *self.use(f"pb{bank}")],
                                incs=[(sT, 1)])
                        self.guard[f"qb{hp}"] = [(sT, self.n(sT))]
                        self.em("v", "tensor_copy", y_buf.ap()[:, hp, :],
                                pb[bank].ap()[:, :],
                                waits=[(sT, self.n(sT)), *self.use(f"yb{hp}"),
                                       *(self.use("brepW") if (tau == 0 and hp == 0) else ())],
                                incs=[(sV, 1)])
                        self.prod(f"pb{bank}", sV)
                    w_y = (sV, self.n(sV))
                    for co in range(8):
                        bank = 4 + co % 2
                        slot = p0 if co < 4 else p1
                        for cc in range(8):
                            self.em("t", "matmul", pb[bank].ap()[:, :],
                                    wring.ap()[:, slot, cc, (co % 4) * 128:(co % 4 + 1) * 128],
                                    y_buf.ap()[:, cc, :],
                                    start=(cc == 0), stop=(cc == 7),
                                    waits=([w_y, *self.use(f"ring{p0}_ld"),
                                            *self.use(f"ring{p1}_ld"),
                                            *self.use(f"pb{bank}")] if cc == 0 else ()),
                                    incs=([(sT, 1)] if cc == 7 else ()))
                        if co == 7:
                            for hp in range(8):
                                self.guard[f"yb{hp}"] = [(sT, self.n(sT))]
                        self.em("v", "scalar_tensor_tensor",
                                x_attn.ap()[:, co, tau * 512:(tau + 1) * 512],
                                pb[bank].ap()[:, :],
                                sm.ap()[:, 40 + co:41 + co],
                                x_attn.ap()[:, co, tau * 512:(tau + 1) * 512],
                                op0=ALU.add, op1=ALU.add,
                                waits=[(sT, self.n(sT)), w_bcol], incs=[(sV, 1)])
                        self.prod(f"pb{bank}", sV)
                    self.guard[f"xa{tau}"] = [(sV, self.n(sV))]
                    if i < 3:
                        inject_expert()
                self.guard["Sbd"] = [(sT, self.n(sT))]
                self.guard["bcol"] = [(sV, self.n(sV))]
                self.guard[f"ring{p0}"] = [(sT, self.n(sT))]
                self.guard[f"ring{p1}"] = [(sT, self.n(sT))]
                self.guard["bqpU"] = [(sT, self.n(sT))]
                if i + 1 < self.n_layers:
                    load_bqp(i + 1)

                if "xa" in self.taps and i == self.n_layers - 1:
                    self.dma("y", dTap, taps["d_xa"][:, :, :], x_attn.ap()[:, :, :],
                             res_waits=[(sV, self.n(sV))])
                if "S" in self.taps and i == 0:
                    self.dma("y", dTap, taps["d_S"][:, :], S_sb.ap()[:, :],
                             res_waits=[(sV, self.n(sV))])

            while exp_step[0] < 12:
                inject_expert()

            if not self.endgame:
                self.em("v", "engine_nop", incs=[(sV, 1)])
                self.dma("y", dOu[0], d["o"][0:128, :], x_attn.ap()[:, 0, 0:1024],
                         res_waits=[(sV, self.n(sV))])
                self.em("y", "wait_ge", dOu[0], self.n(dOu[0]))
                if self.n(dTap):
                    self.em("y", "wait_ge", dTap, self.n(dTap))
                self.finish(block)
                return d, taps

            self.join_all()

            pfl = self.dma("y", dPf, phi_full.ap()[:, :, :],
                           phibf.rearrange("(cc p) e -> p cc e", p=128),
                           res_waits=[w_phicast])
            w_pf = (dPf, pfl)

            def emit_ct_transposes(tg, tloc):
                cpar = (4 * tg + tloc) % 3
                w_C = self.use(f"CbufR{cpar}")
                for k in range(16):
                    bank = 4 + k % 4
                    bfv = pb[bank].ap()[:, :].bitcast(BF16)
                    self.em("t", "transpose", bfv[:, 0:128],
                            Cbuf.ap()[:, cpar, k * 128:(k + 1) * 128], idbf.ap(),
                            waits=[*w_C, *self.use(f"pbt{bank}"),
                                   *(self.use(f"pb{bank}") if k < 4 else ())],
                            incs=[(sT, 1)])
                    self.em("v", "tensor_copy",
                            Ct.ap()[:, k, tloc * 128:(tloc + 1) * 128],
                            bfv[:, 0:128],
                            waits=[(sT, self.n(sT)),
                                   *(self.use("Ct") if k == 0 else ())],
                            incs=[(sV, 1)])
                    self.guard[f"pbt{bank}"] = [(sV, self.n(sV))]
                self.guard[f"Cbuf{cpar}"] = [(sT, self.n(sT))]

            for tg in range(4):
                for tloc in range(4):
                    tt = 4 * tg + tloc
                    tpar = tt % 2
                    cpar = tt % 3
                    xld = self.dma("y", dXn[tpar], xntau.ap()[:, tpar],
                                   xn_spill[:, :, tt * 128:(tt + 1) * 128],
                                   res_waits=self.use(f"xnt{tpar}"))
                    w_xn = (dXn[tpar], xld)
                    for ch in range(4):
                        for cc in range(8):
                            self.em("t", "matmul", pb[ch].ap()[:, :],
                                    xntau.ap()[:, tpar, cc, :],
                                    phi_full.ap()[:, cc, ch * 512:(ch + 1) * 512],
                                    start=(cc == 0), stop=False,
                                    waits=([w_pf, w_xn, *self.use(f"pb{ch}")]
                                           if cc == 0 else ()))
                        self.em("t", "matmul", pb[ch].ap()[:, :],
                                ones.ap()[0:1, 0:128],
                                phib_bf.ap()[0:1, ch * 512:(ch + 1) * 512],
                                start=False, stop=True,
                                incs=([(sT, 1)] if ch == 3 else ()))
                    self.guard[f"xnt{tpar}"] = [(sT, self.n(sT))]
                    w_log = (sT, self.n(sT))
                    for ch in range(4):
                        self.em("v", "reduce_max", sm.ap()[:, 16 + ch:17 + ch],
                                pb[ch].ap()[:, :], axis=AX.X, negate=True,
                                waits=([w_log, *self.use("smE")] if ch == 0 else ()),
                                incs=[(sV, 1)])
                    self.em("v", "tensor_reduce", sm.ap()[:, 20:21],
                            sm.ap()[:, 16:20], axis=AX.X, op=ALU.min,
                            waits=[self.vsem()], incs=[(sV, 1)])
                    w_max = (sV, self.n(sV))
                    for ch in range(4):
                        self.em("s", "activation", Ebuf.ap()[:, ch * 512:(ch + 1) * 512],
                                pb[ch].ap()[:, :], AF.Exp, bias=sm.ap()[:, 20:21],
                                waits=([w_max, *self.use("Ebuf")] if ch == 0 else ()),
                                incs=[(sS, 1)])
                        self.prod(f"pb{ch}", sS)
                    self.prod("smE", sS)
                    w_exp = (sS, self.n(sS))
                    self.em("v", "reduce_sum",
                            sm.ap()[:, 24:56].rearrange("q (p o) -> q p o", o=1),
                            Ebuf.ap()[:, :].rearrange("q (n p) -> q p n", p=32),
                            axis=AX.X, waits=[w_exp], incs=[(sV, 1)])
                    self.em("v", "reciprocal", sm.ap()[:, 24:56], sm.ap()[:, 24:56],
                            waits=[self.vsem()], incs=[(sV, 1)])
                    self.em("v", "tensor_tensor",
                            Cbuf.ap()[:, cpar, :].rearrange("q (n p) -> q n p", p=32),
                            Ebuf.ap()[:, :].rearrange("q (n p) -> q n p", p=32),
                            sm.ap()[:, 24:56][:, None, :].broadcast_to([128, 64, 32]),
                            op=ALU.mult,
                            waits=[self.vsem(), *self.use(f"Cbuf{cpar}")],
                            incs=[(sV, 1)])
                    self.prod("Ebuf", sV)
                    self.guard[f"CbufR{cpar}"] = [(sV, self.n(sV))]
                    if tloc >= 2:
                        emit_ct_transposes(tg, tloc - 2)
                emit_ct_transposes(tg, 2)
                emit_ct_transposes(tg, 3)
                w_ct_all = (sV, self.n(sV))
                for npc in range(16):
                    ypar = npc % 2
                    row0 = 256 * npc if npc < 8 else 256 * (npc - 8) + 128
                    yld = self.dma("y", dYt[ypar], yt2.ap()[:, ypar],
                                   ag_out[row0:row0 + 128, :],
                                   res_waits=[(ccAG, 1), *self.use(f"yt2{ypar}")])
                    w_yt = (dYt[ypar], yld)
                    for ch in range(8):
                        self.em("t", "matmul", pb[ch].ap()[:, :],
                                yt2.ap()[:, ypar, ch * 128:(ch + 1) * 128],
                                Ct.ap()[:, npc, :],
                                start=(npc == 0), stop=(npc == 15),
                                waits=([w_yt, w_ct_all, *self.use(f"pb{ch}"),
                                        *self.use("pbt4"), *self.use("pbt5"),
                                        *self.use("pbt6"), *self.use("pbt7")]
                                       if npc == 0 else [w_yt]),
                                incs=([(sT, 1)] if ch == 7 else ()))
                    self.guard[f"yt2{ypar}"] = [(sT, self.n(sT))]
                self.guard["Ct"] = [(sT, self.n(sT))]
                w_Y = (sT, self.n(sT))
                for ch in range(8):
                    self.em("s", "activation", th_s.ap()[:, :], pb[ch].ap()[:, :],
                            AF.Tanh, scale=cst.ap()[:, 1:2],
                            waits=[w_Y, cst_dn, *self.use("th_s")], incs=[(sS, 1)])
                    w_th = (sS, self.n(sS))
                    self.em("v", "scalar_tensor_tensor", gs_buf.ap()[:, :],
                            th_s.ap()[:, :], 1.0, pb[ch].ap()[:, :],
                            op0=ALU.add, op1=ALU.mult,
                            waits=[w_th, *self.use("gs_buf")], incs=[(sV, 1)])
                    self.prod("th_s", sV)
                    self.prod(f"pb{ch}", sV)
                    self.em("v", "scalar_tensor_tensor", sgx.ap()[:, ch, :],
                            gs_buf.ap()[:, :], 0.5,
                            x_attn.ap()[:, ch, tg * 512:(tg + 1) * 512],
                            op0=ALU.mult, op1=ALU.mult,
                            waits=[self.vsem()], incs=[(sV, 1)])
                    self.prod("gs_buf", sV)
                    self.guard[f"sgc{ch}"] = [(sV, self.n(sV))]
                for tloc in range(4):
                    tt = 4 * tg + tloc
                    opar = tt % 2
                    xld = self.dma("y", dXi[opar], xo.ap()[:, opar],
                                   d["x"][tt * 128:(tt + 1) * 128, :],
                                   res_waits=self.use(f"xo{opar}"))
                    w_xo = (dXi[opar], xld)
                    for rnd in range(2):
                        for k in range(4):
                            ch = 4 * rnd + k
                            bank = 4 + k
                            self.em("t", "transpose", pb[bank].ap()[:, 0:128],
                                    sgx.ap()[:, ch, tloc * 128:(tloc + 1) * 128],
                                    idf32.ap(),
                                    waits=[*self.use(f"sgc{ch}"),
                                           *self.use(f"pbe{bank}"),
                                           *(self.use(f"pb{bank}") if (tloc == 0 and rnd == 0) else ())],
                                    incs=[(sT, 1)])
                            self.em("v", "tensor_tensor",
                                    outt.ap()[:, opar, ch * 128:(ch + 1) * 128],
                                    pb[bank].ap()[:, 0:128],
                                    xo.ap()[:, opar, ch * 128:(ch + 1) * 128],
                                    op=ALU.add,
                                    waits=[(sT, self.n(sT)), w_xo,
                                           *(self.use(f"outt{opar}") if ch == 0 else ())],
                                    incs=[(sV, 1)])
                            self.guard[f"pbe{bank}"] = [(sV, self.n(sV))]
                    st = self.dma("y", dOu[opar], d["o"][tt * 128:(tt + 1) * 128, :],
                                  outt.ap()[:, opar], res_waits=[(sV, self.n(sV))])
                    self.guard[f"outt{opar}"] = [(dOu[opar], st)]
                    self.guard[f"xo{opar}"] = [(sV, self.n(sV))]
                for ch in range(4):
                    self.guard[f"pb{4 + ch}"] = self.guard[f"pbe{4 + ch}"]

            self.em("y", "wait_ge", dOu[0], self.n(dOu[0]))
            self.em("y", "wait_ge", dOu[1], self.n(dOu[1]))
            if self.n(dTap):
                self.em("y", "wait_ge", dTap, self.n(dTap))
            self.finish(block)
        return d, taps

    def finish(self, block):
        streams = self.streams

        @block.tensor
        def _(e):
            for fn in streams["t"]:
                fn(e)

        @block.vector
        def _(e):
            for fn in streams["v"]:
                fn(e)

        @block.scalar
        def _(e):
            for fn in streams["s"]:
                fn(e)

        @block.gpsimd
        def _(e):
            for fn in streams["g"]:
                fn(e)

        @block.sync
        def _(e):
            for fn in streams["y"]:
                fn(e)



def _pack_inputs(L0_attn_inputs_embeds, L0_attn_Wqkv, L0_attn_bqkv, L0_attn_Wproj,
                 L0_attn_bproj, L0_ffn_phi_w, L0_ffn_phi_b, L0_ffn_expert_w,
                 L0_ffn_expert_b):
    x = np.ascontiguousarray(np.asarray(L0_attn_inputs_embeds, dtype=np.float32)
                             .reshape(2 * 8192, C))
    Wqkv = np.asarray(L0_attn_Wqkv, dtype=np.float32)
    Wproj = np.asarray(L0_attn_Wproj, dtype=np.float32)
    bqkv = np.ascontiguousarray(np.asarray(L0_attn_bqkv, dtype=np.float32))
    bproj = np.ascontiguousarray(np.asarray(L0_attn_bproj, dtype=np.float32))
    phiT = np.ascontiguousarray(np.asarray(L0_ffn_phi_w, dtype=np.float32).T)
    phib = np.ascontiguousarray(np.asarray(L0_ffn_phi_b, dtype=np.float32).reshape(1, NP))
    EWall = np.asarray(L0_ffn_expert_w, dtype=np.float32)
    EBall = np.asarray(L0_ffn_expert_b, dtype=np.float32)

    WA = np.empty((NLAYER, 8, C, 512), dtype=np.float32)
    for l in range(NLAYER):
        W1 = Wqkv[l].T
        WP = Wproj[l].T
        WA[l, 0] = W1[:, 1024:1536]
        WA[l, 1] = W1[:, 2048:2560]
        WA[l, 2] = W1[:, 1536:2048]
        WA[l, 3] = W1[:, 2560:3072]
        WA[l, 4] = W1[:, 0:512]
        WA[l, 5] = W1[:, 512:1024]
        WA[l, 6] = WP[:, 0:512]
        WA[l, 7] = WP[:, 512:1024]
    WA = np.ascontiguousarray(WA)

    ones = np.ones((1, 512), dtype=BF)
    idbf = np.eye(128, dtype=BF)
    idf32 = np.eye(128, dtype=np.float32)

    in_maps = []
    for c in range(NCORES):
        experts = [4 * c + j for j in range(4)] + [32 + 4 * c + j for j in range(4)]
        in_maps.append({
            "x": np.ascontiguousarray(x[c * T:(c + 1) * T]),
            "WA": WA, "bqkv": bqkv, "bproj": bproj,
            "phiT": phiT, "phib": phib,
            "EW": np.ascontiguousarray(EWall[experts]),
            "EB": np.ascontiguousarray(EBall[experts]),
            "ones": ones, "idbf": idbf, "idf32": idf32,
        })
    return in_maps


_CACHE = {}


def _get_builder():
    if "kb" not in _CACHE:
        kb = KB()
        kb.build()
        _CACHE["kb"] = kb
    return _CACHE["kb"]


def kernel(**inputs):
    kb = _get_builder()
    in_maps = _pack_inputs(**inputs)
    res = run_bass_kernel_spmd(kb.nc, in_maps, core_ids=list(range(NCORES)))
    out = np.concatenate([np.asarray(r["o"]) for r in res.results], axis=0)
    return np.ascontiguousarray(out.reshape(2, 8192, C).astype(np.float32, copy=False))


# version 08
# speedup vs baseline: 1.1685x; 1.1685x over previous
import numpy as np
import ml_dtypes
import concourse.bass as bass
import concourse.mybir as mybir
from concourse.bass_utils import run_bass_kernel_spmd
from contextlib import ExitStack

F32 = mybir.dt.float32
BF16 = mybir.dt.bfloat16
AF = mybir.ActivationFunctionType
ALU = mybir.AluOpType
AX = mybir.AxisListType
BF = ml_dtypes.bfloat16

NCORES = 8
T = 2048
C = 1024
NLAYER = 12
LN_EPS = 1e-5
NP = 2048
SBUF_TOP = 229376


class KB:
    def __init__(self, n_layers=NLAYER, endgame=True, taps=(), upto="full"):
        self.n_layers = n_layers
        self.endgame = endgame
        self.upto = upto
        self.taps = set(taps)
        self.nc = bass.Bass("TRN2", target_bir_lowering=False, debug=False)
        self.streams = {"t": [], "v": [], "s": [], "g": [], "y": []}
        self.cnt = {}
        self.hw = {}
        self.guard = {}
        self.ov = SBUF_TOP - 114688 - 32

    def em(self, eng, method, *args, waits=(), incs=(), **kw):
        real_waits = []
        for wt in waits:
            if wt is None:
                continue
            sem, val = wt
            key = (eng, sem.name)
            if self.hw.get(key, -1) >= val:
                continue
            self.hw[key] = val
            real_waits.append((sem, val))
        for sem, amt in incs:
            self.cnt[sem.name] = self.cnt.get(sem.name, 0) + amt

        def fn(e, method=method, args=args, kw=kw, rw=tuple(real_waits), incs=tuple(incs)):
            for sem, val in rw:
                e.wait_ge(sem, val)
            inst = getattr(e, method)(*args, **kw)
            for sem, amt in incs:
                inst.then_inc(sem, amt)
        self.streams[eng].append(fn)

    def n(self, sem):
        return self.cnt.get(sem.name, 0)

    def use(self, res):
        return self.guard.get(res, [])

    def prod(self, res, sem):
        self.guard[res] = [(sem, self.n(sem))]

    def vsem(self):
        return (self.sV, self.n(self.sV))

    def join_all(self):
        sems = [self.sT, self.sV, self.sS] + list(self.dma_sems)
        vals = [(s, self.n(s)) for s in sems]
        for eng in ("t", "v", "s", "g", "y"):
            for s, v in vals:
                if v > 0:
                    self.em(eng, "wait_ge", s, v)

    def dma(self, eng, sem, out, in_, res_waits=(), **kw):
        self.em(eng, "dma_start", waits=res_waits, incs=[(sem, 16)],
                out=out, in_=in_, **kw)
        return self.n(sem)

    def build(self):
        nc = self.nc
        ov0 = self.ov

        d = {}
        d["x"] = nc.dram_tensor("x", [T, C], F32, kind="ExternalInput").ap()
        d["WA"] = nc.dram_tensor("WA", [NLAYER, 8, C, 512], F32, kind="ExternalInput").ap()
        d["bqkv"] = nc.dram_tensor("bqkv", [NLAYER, 3 * C], F32, kind="ExternalInput").ap()
        d["bproj"] = nc.dram_tensor("bproj", [NLAYER, C], F32, kind="ExternalInput").ap()
        d["phiT"] = nc.dram_tensor("phiT", [C, NP], F32, kind="ExternalInput").ap()
        d["phib"] = nc.dram_tensor("phib", [1, NP], F32, kind="ExternalInput").ap()
        d["EW"] = nc.dram_tensor("EW", [8, C, C], F32, kind="ExternalInput").ap()
        d["EB"] = nc.dram_tensor("EB", [8, C], F32, kind="ExternalInput").ap()
        d["ones"] = nc.dram_tensor("ones", [1, 512], BF16, kind="ExternalInput").ap()
        d["idbf"] = nc.dram_tensor("idbf", [128, 128], BF16, kind="ExternalInput").ap()
        d["idf32"] = nc.dram_tensor("idf32", [128, 128], F32, kind="ExternalInput").ap()
        d["o"] = nc.dram_tensor("o", [T, C], F32, kind="ExternalOutput").ap()

        xn_spill = nc.dram_tensor("xn_spill", [128, 8, T], BF16).ap()
        s_in = nc.dram_tensor("s_in", [128, 512], F32).ap()
        s_out = nc.dram_tensor("s_out", [128, 512], F32).ap()
        rs_in = [nc.dram_tensor(f"rs_in{h}", [1024, C], F32).ap() for h in range(2)]
        rs_out = [nc.dram_tensor(f"rs_out{h}", [128, C], F32).ap() for h in range(2)]
        ag_in = nc.dram_tensor("ag_in", [256, C], BF16).ap()
        ag_out = nc.dram_tensor("ag_out", [NP, C], BF16, addr_space="Shared").ap()

        taps = {}
        if "xn" in self.taps:
            taps["d_xn"] = nc.dram_tensor("d_xn", [128, 8, T], F32, kind="ExternalOutput").ap()
        if "xhalf" in self.taps:
            taps["d_xh"] = nc.dram_tensor("d_xh", [128, 8, C], F32, kind="ExternalOutput").ap()
        if "rs" in self.taps:
            taps["d_rs"] = nc.dram_tensor("d_rs", [2, 128, C], F32, kind="ExternalOutput").ap()
        if "yt" in self.taps:
            taps["d_yt"] = nc.dram_tensor("d_yt", [128, 2, C], BF16, kind="ExternalOutput").ap()
        if "S" in self.taps:
            taps["d_S"] = nc.dram_tensor("d_S", [128, 512], F32, kind="ExternalOutput").ap()
        if "xa" in self.taps:
            taps["d_xa"] = nc.dram_tensor("d_xa", [128, 8, T], F32, kind="ExternalOutput").ap()

        with ExitStack() as ctx:
            ecx = ctx.enter_context

            x_attn = ecx(nc.sbuf_tensor("x_attn", [128, 8, T], F32))
            ones = ecx(nc.sbuf_tensor("ones_sb", [1, 512], BF16))
            idbf = ecx(nc.sbuf_tensor("idbf_sb", [128, 128], BF16))
            idf32 = ecx(nc.sbuf_tensor("idf32_sb", [128, 128], F32))
            phib_bf = ecx(nc.sbuf_tensor("phib_bf", [1, NP], BF16))
            S_sb = ecx(nc.sbuf_tensor("S_sb", [128, 512], F32))
            S_bd = ecx(nc.sbuf_tensor("S_bd", [128, 8, 128], BF16))
            sm = ecx(nc.sbuf_tensor("sm", [128, 64], F32))
            cst = ecx(nc.sbuf_tensor("cst", [128, 2], F32))
            zer = ecx(nc.sbuf_tensor("zer", [1, 128], BF16))

            AT = nc.alloc_sbuf_tensor_at
            o = ov0
            xin = AT("xin", [128, 2, C], F32, offset=o); o += 8192
            xf = AT("xf", [128, 2, C], F32, offset=o); o += 8192
            xsq = AT("xsq", [128, 2, C], F32, offset=o); o += 8192
            xn_tc = AT("xn_tc", [128, 16, C], BF16, offset=o); o += 32768
            xnct_st = AT("xnct_st", [128, 2, 8, 128], BF16, offset=o); o += 4096
            X_half = AT("X_half", [128, 8, C], F32, offset=o); o += 32768
            phi_sl = AT("phi_sl", [128, 2, 8, 384], BF16, offset=o); o += 12288
            E2 = AT("E2", [128, 2, 384], F32, offset=o); o += 3072
            D2 = AT("D2", [128, 4, 384], BF16, offset=o); o += 3072
            gs12 = AT("gs12", [128, 2, 12], F32, offset=o); o += 128
            assert o <= SBUF_TOP - 32, f"phase01 overlay {o}"

            o = ov0
            wring = AT("wring", [128, 4, 8, 512], BF16, offset=o); o += 32768
            staging = AT("staging", [128, 2, 8, 512], BF16, offset=o); o += 16384
            kv_buf = AT("kv_buf", [128, 2, 4, 1024], BF16, offset=o); o += 16384
            q_buf = AT("q_buf", [128, 8, 512], BF16, offset=o); o += 8192
            y_buf = AT("y_buf", [128, 8, 512], BF16, offset=o); o += 8192
            bkv_bf = AT("bkv_bf", [1, 2048], BF16, offset=o); o += 4096
            bqp_bf = AT("bqp_bf", [1, 2048], BF16, offset=o); o += 4096
            EWq = AT("EWq", [128, 2, 8, 256], BF16, offset=o); o += 8192
            xt = AT("xt", [128, C], F32, offset=o); o += 4096
            xtb = AT("xtb", [128, C], BF16, offset=o); o += 2048
            Xh = AT("Xh", [128, 2, 8, 128], BF16, offset=o); o += 4096
            yt_sl = AT("yt_sl", [128, 2, C], BF16, offset=o); o += 4096
            eb_bf = AT("eb_bf", [1, 1, C], BF16, offset=o); o += 2048
            assert o <= SBUF_TOP - 32, f"attn overlay {o}"
            wring5 = (EWq.ap()[:, :, :, :].rearrange("q a b c -> q (a b c)")
                      .rearrange("q (b f) -> q b f", f=512))
            exsq = kv_buf.ap()[:, 0, 0:2, :].bitcast(F32).rearrange("q a b -> q (a b)")

            o = ov0
            phi_full = AT("phi_full", [128, 8, NP], BF16, offset=o); o += 32768
            Ct = AT("Ct", [128, 16, 512], BF16, offset=o); o += 16384
            Ebuf = AT("Ebuf", [128, NP], F32, offset=o); o += 8192
            Cbuf = AT("Cbuf", [128, 3, NP], BF16, offset=o); o += 12288
            th_s = AT("th_s", [128, 512], F32, offset=o); o += 2048
            gs_buf = AT("gs_buf", [128, 512], F32, offset=o); o += 2048
            sgx = AT("sgx", [128, 8, 512], F32, offset=o); o += 16384
            outt = AT("outt", [128, 2, C], F32, offset=o); o += 8192
            xo = AT("xo", [128, 2, C], F32, offset=o); o += 8192
            yt2 = AT("yt2", [128, 2, C], BF16, offset=o); o += 4096
            assert o <= SBUF_TOP - 32, f"endgame overlay {o}"
            xntau = AT("xntau", [128, 2, 8, 128], BF16, offset=ov0 - 4096)

            pb = [ecx(nc.psum_tensor(f"pb{i}", [128, 512], F32)) for i in range(8)]

            self.sT = ecx(nc.semaphore("sT"))
            self.sV = ecx(nc.semaphore("sV"))
            self.sS = ecx(nc.semaphore("sS"))
            ccS = [ecx(nc.semaphore(f"ccS{i}")) for i in range(self.n_layers)]
            ccRS = [ecx(nc.semaphore("ccRS0")), ecx(nc.semaphore("ccRS1"))]
            ccAG = ecx(nc.semaphore("ccAG"))
            sT, sV, sS = self.sT, self.sV, self.sS
            dCst = ecx(nc.semaphore("dCst"))
            dXi = [ecx(nc.semaphore(f"dXi{p}")) for p in range(2)]
            dSp = [ecx(nc.semaphore(f"dSp{p}")) for p in range(2)]
            dXn = [ecx(nc.semaphore(f"dXn{p}")) for p in range(2)]
            dPh = [ecx(nc.semaphore(f"dPh{p}")) for p in range(2)]
            dXh = ecx(nc.semaphore("dXh"))
            dW = [ecx(nc.semaphore(f"dW{p}")) for p in range(4)]
            dBk = ecx(nc.semaphore("dBk"))
            dBq = ecx(nc.semaphore("dBq"))
            dXt = ecx(nc.semaphore("dXt"))
            dEb = ecx(nc.semaphore("dEb"))
            dEw = [ecx(nc.semaphore(f"dEw{p}")) for p in range(2)]
            dAg = ecx(nc.semaphore("dAg"))
            dSio = ecx(nc.semaphore("dSio"))
            dPf = ecx(nc.semaphore("dPf"))
            dYt = [ecx(nc.semaphore(f"dYt{p}")) for p in range(2)]
            dOu = [ecx(nc.semaphore(f"dOu{p}")) for p in range(2)]
            dTap = ecx(nc.semaphore("dTap"))
            self.dma_sems = ([dCst, dXh, dBk, dBq, dXt, dEb, dAg, dSio,
                              dPf, dTap] + dSp + dXi + dXn + dPh + dW + dEw + dYt + dOu)

            block = ecx(nc.Block())

            self.dma("g", dCst, ones.ap(), d["ones"])
            self.dma("g", dCst, idbf.ap(), d["idbf"])
            self.dma("g", dCst, idf32.ap(), d["idf32"])
            self.dma("g", dCst, phib_bf.ap(), d["phib"])
            consts_dn = (dCst, self.n(dCst))
            self.em("v", "memset", cst.ap()[:, 0:1], float(LN_EPS))
            self.em("v", "memset", zer.ap()[:, :], 0.0)
            self.em("v", "memset", cst.ap()[:, 1:2], 0.5, incs=[(sV, 1)])
            cst_dn = (sV, self.n(sV))

            one_c = 1.0 / C

            def ln_stats(src_ap, w_src, sq_ap, sq_guard_key):
                self.em("v", "tensor_mul", sq_ap, src_ap, src_ap,
                        waits=[w_src, *self.use(sq_guard_key), *self.use("sm")],
                        incs=[(sV, 1)])
                w_mul = self.vsem()
                self.em("v", "reduce_sum", sm.ap()[:, 0:1], src_ap, axis=AX.X,
                        negate=True, incs=[(sV, 1)])
                self.em("v", "reduce_sum", sm.ap()[:, 1:2], sq_ap, axis=AX.X,
                        waits=[w_mul], incs=[(sV, 1)])
                self.prod(sq_guard_key, sV)
                self.em("v", "tensor_scalar_mul", sm.ap()[:, 2:3], sm.ap()[:, 0:1],
                        one_c, waits=[self.vsem()], incs=[(sV, 1)])
                self.em("v", "tensor_scalar_mul", sm.ap()[:, 3:4], sm.ap()[:, 1:2],
                        one_c, incs=[(sV, 1)])
                self.em("v", "tensor_mul", sm.ap()[:, 4:5], sm.ap()[:, 2:3],
                        sm.ap()[:, 2:3], waits=[self.vsem()], incs=[(sV, 1)])
                self.em("v", "tensor_sub", sm.ap()[:, 5:6], sm.ap()[:, 3:4],
                        sm.ap()[:, 4:5], waits=[self.vsem()], incs=[(sV, 1)])
                self.em("s", "activation", sm.ap()[:, 6:7], sm.ap()[:, 5:6], AF.Sqrt,
                        bias=cst.ap()[:, 0:1], waits=[self.vsem(), cst_dn],
                        incs=[(sS, 1)])
                self.em("v", "reciprocal", sm.ap()[:, 7:8], sm.ap()[:, 6:7],
                        waits=[(sS, self.n(sS))], incs=[(sV, 1)])
                self.em("v", "tensor_mul", sm.ap()[:, 8:9], sm.ap()[:, 2:3],
                        sm.ap()[:, 7:8], waits=[self.vsem()], incs=[(sV, 1)])
                return (sV, self.n(sV))

            for tt in range(16):
                par = tt % 2
                ld = self.dma("y", dXi[par], xin.ap()[:, par],
                              d["x"][tt * 128:(tt + 1) * 128, :],
                              res_waits=self.use(f"xin{par}"))
                xi = xin.ap()[:, par]
                self.em("v", "tensor_mul", xsq.ap()[:, par], xi, xi,
                        waits=[(dXi[par], ld), *self.use(f"xsq{par}")],
                        incs=[(sV, 1)])
                w_mul = self.vsem()
                self.em("v", "reduce_sum", sm.ap()[:, tt:tt + 1], xi, axis=AX.X,
                        negate=True, incs=[(sV, 1)])
                self.guard[f"xin{par}"] = [(sV, self.n(sV))]
                self.em("v", "reduce_sum", sm.ap()[:, 16 + tt:17 + tt],
                        xsq.ap()[:, par], axis=AX.X, waits=[w_mul], incs=[(sV, 1)])
                self.prod(f"xsq{par}", sV)
            self.em("v", "tensor_scalar_mul", sm.ap()[:, 0:16], sm.ap()[:, 0:16],
                    one_c, waits=[self.vsem()], incs=[(sV, 1)])
            self.em("v", "tensor_scalar_mul", sm.ap()[:, 16:32], sm.ap()[:, 16:32],
                    one_c, incs=[(sV, 1)])
            self.em("v", "tensor_mul", sm.ap()[:, 32:48], sm.ap()[:, 0:16],
                    sm.ap()[:, 0:16], waits=[self.vsem()], incs=[(sV, 1)])
            self.em("v", "tensor_sub", sm.ap()[:, 16:32], sm.ap()[:, 16:32],
                    sm.ap()[:, 32:48], waits=[self.vsem()], incs=[(sV, 1)])
            self.em("s", "activation", sm.ap()[:, 32:48], sm.ap()[:, 16:32], AF.Sqrt,
                    bias=cst.ap()[:, 0:1], waits=[self.vsem(), cst_dn], incs=[(sS, 1)])
            self.em("v", "reciprocal", sm.ap()[:, 48:64], sm.ap()[:, 32:48],
                    waits=[(sS, self.n(sS))], incs=[(sV, 1)])
            self.em("v", "tensor_mul", sm.ap()[:, 32:48], sm.ap()[:, 0:16],
                    sm.ap()[:, 48:64], waits=[self.vsem()], incs=[(sV, 1)])
            w_stats = (sV, self.n(sV))
            w_stats_s = (sS, self.n(sS))
            for tt in range(16):
                par = tt % 2
                ld = self.dma("y", dXi[par], xin.ap()[:, par],
                              d["x"][tt * 128:(tt + 1) * 128, :],
                              res_waits=self.use(f"xin{par}"))
                xi = xin.ap()[:, par]
                w_ld = (dXi[par], ld)
                self.em("s", "activation", xn_tc.ap()[:, tt, :], xi, AF.Identity,
                        bias=sm.ap()[:, 32 + tt:33 + tt],
                        scale=sm.ap()[:, 48 + tt:49 + tt],
                        waits=[w_ld, w_stats], incs=[(sS, 1)])
                w_xnb = (sS, self.n(sS))
                self.em("v", "tensor_scalar", xf.ap()[:, par], xi,
                        sm.ap()[:, tt:tt + 1], sm.ap()[:, 48 + tt:49 + tt],
                        op0=ALU.add, op1=ALU.mult,
                        waits=[w_ld, w_stats, *self.use(f"xf{par}")], incs=[(sV, 1)])
                w_xnf = (sV, self.n(sV))
                self.guard[f"xin{par}"] = [(sV, self.n(sV)), (sS, self.n(sS))]
                for j in range(8):
                    bfv = pb[j].ap()[:, :].bitcast(BF16)
                    self.em("t", "transpose", bfv[:, 0:128],
                            xn_tc.ap()[:, tt, j * 128:(j + 1) * 128], idbf.ap(),
                            waits=[w_xnb, consts_dn, *self.use(f"pb{j}")],
                            incs=([(sT, 1)] if j == 7 else ()))
                w_trb = (sT, self.n(sT))
                spar = tt % 2
                for j in range(8):
                    bfv = pb[j].ap()[:, :].bitcast(BF16)
                    self.em("v", "tensor_copy", xnct_st.ap()[:, spar, j, :],
                            bfv[:, 0:128],
                            waits=([w_trb, *self.use(f"xnct{spar}")] if j == 0 else ()),
                            incs=([(sV, 1)] if j == 7 else ()))
                w_ctev = (sV, self.n(sV))
                for j in range(8):
                    self.em("t", "transpose", pb[j].ap()[:, 0:128],
                            xf.ap()[:, par, j * 128:(j + 1) * 128], idf32.ap(),
                            waits=[w_xnf, w_ctev],
                            incs=([(sT, 1)] if j == 7 else ()))
                w_trf = (sT, self.n(sT))
                self.prod(f"xf{par}", sT)
                for j in range(8):
                    self.em("v", "tensor_copy",
                            x_attn.ap()[:, j, tt * 128:(tt + 1) * 128],
                            pb[j].ap()[:, 0:128],
                            waits=([w_trf] if j == 0 else ()),
                            incs=([(sV, 1)] if j == 7 else ()))
                for j in range(8):
                    self.prod(f"pb{j}", sV)
                sp = self.dma("y", dSp[spar], xn_spill[:, :, tt * 128:(tt + 1) * 128],
                              xnct_st.ap()[:, spar], res_waits=[w_ctev])
                self.guard[f"xnct{spar}"] = [(dSp[spar], sp)]

            if "xn" in self.taps:
                self.em("v", "engine_nop", incs=[(sV, 1)])
                self.dma("y", dTap, taps["d_xn"][:, :, :], x_attn.ap()[:, :, :],
                         res_waits=[(sV, self.n(sV))])
            if self.upto == "p0":
                self.em("v", "engine_nop", incs=[(sV, 1)])
                self.dma("y", dOu[0], d["o"][0:128, :], x_attn.ap()[:, 0, 0:1024],
                         res_waits=[(sV, self.n(sV))])
                self.em("y", "wait_ge", dOu[0], self.n(dOu[0]))
                self.em("y", "wait_ge", dTap, self.n(dTap)) if self.n(dTap) else None
                self.finish(block)
                return d, taps

            groups = [(0, 2), (2, 2), (4, 2), (6, 2)]
            lgbank = [0, 1, 6, 7]
            phase0_done = [(sV, self.n(sV)), (sS, self.n(sS)), (sT, self.n(sT))]
            gi_all = 0
            for H in range(2):
                for (b0g, gsz) in groups:
                    gpar = gi_all % 2
                    cols = gsz * 128
                    e0 = (8 * H + b0g) * 128
                    ng = cols // 32
                    pld = self.dma(
                        "g", dPh[gpar], phi_sl.ap()[:, gpar, :, 0:cols],
                        d["phiT"].rearrange("(cc p) e -> p cc e", p=128)[:, :, e0:e0 + cols],
                        res_waits=self.use(f"phi{gpar}"))
                    w_phi = (dPh[gpar], pld)
                    if gi_all < 3 and self.n_layers > 0:
                        r = gi_all
                        ld0 = self.dma("g", dW[r], wring.ap()[:, r],
                                       d["WA"][0, r].rearrange("(cc p) f -> p cc f", p=128),
                                       res_waits=phase0_done)
                        self.guard[f"ring{r}_ld"] = [(dW[r], ld0)]


                    def emit_xacc(tt, w_D):
                        for j in range(gsz):
                            for hd in range(2):
                                bank = 2 + j * 2 + hd
                                last = (j == gsz - 1 and hd == 1)
                                self.em("t", "matmul", pb[bank].ap()[:, :],
                                        D2.ap()[:, tt % 4, j * 128:(j + 1) * 128],
                                        xn_tc.ap()[:, tt, hd * 512:(hd + 1) * 512],
                                        start=(tt == 0), stop=(tt == 15),
                                        waits=([w_D, *self.use(f"pb{bank}")]
                                               if tt == 0 else [w_D]),
                                        incs=([(sT, 1)] if (last or tt == 15) else ()))
                        self.guard[f"D2{tt % 4}"] = [(sT, self.n(sT))]

                    wds = {}
                    for tt in range(16):
                        tpar = tt % 2
                        lb = lgbank[tt % 4]
                        xld = self.dma("y", dXn[tpar], xnct_st.ap()[:, tpar],
                                       xn_spill[:, :, tt * 128:(tt + 1) * 128],
                                       res_waits=[(dSp[0], self.n(dSp[0])),
                                                  (dSp[1], self.n(dSp[1])),
                                                  *self.use(f"xnct{tpar}")])
                        w_xn = (dXn[tpar], xld)
                        pl = pb[lb]
                        for cc in range(8):
                            self.em("t", "matmul", pl.ap()[:, 0:cols],
                                    xnct_st.ap()[:, tpar, cc, :],
                                    phi_sl.ap()[:, gpar, cc, 0:cols],
                                    start=(cc == 0), stop=False,
                                    waits=([w_phi, w_xn, *phase0_done,
                                            *self.use(f"pb{lb}")] if cc == 0 else ()))
                        self.em("t", "matmul", pl.ap()[:, 0:cols],
                                ones.ap()[0:1, 0:128], phib_bf.ap()[0:1, e0:e0 + cols],
                                start=False, stop=True, incs=[(sT, 1)])
                        self.guard[f"xnct{tpar}"] = [(sT, self.n(sT))]
                        w_log = (sT, self.n(sT))
                        if tt >= 2:
                            emit_xacc(tt - 2, wds.pop(tt - 2))
                        self.em("v", "reduce_max", sm.ap()[:, 20:21], pl.ap()[:, 0:cols],
                                axis=AX.X, negate=True,
                                waits=[w_log, *self.use("smE")], incs=[(sV, 1)])
                        self.em("s", "activation", E2.ap()[:, tpar, 0:cols],
                                pl.ap()[:, 0:cols],
                                AF.Exp, bias=sm.ap()[:, 20:21],
                                waits=[(sV, self.n(sV)), *self.use(f"E2{tpar}")],
                                incs=[(sS, 1)])
                        self.prod(f"pb{lb}", sS)
                        self.prod("smE", sS)
                        w_exp = (sS, self.n(sS))
                        self.em("v", "reduce_sum",
                                gs12.ap()[:, tpar, 0:ng, None],
                                E2.ap()[:, tpar, 0:cols].rearrange("q (n p) -> q n p", p=32),
                                axis=AX.X, waits=[w_exp], incs=[(sV, 1)])
                        self.em("v", "reciprocal", gs12.ap()[:, tpar, 0:ng],
                                gs12.ap()[:, tpar, 0:ng],
                                waits=[self.vsem()], incs=[(sV, 1)])
                        self.em("v", "tensor_tensor",
                                D2.ap()[:, tt % 4, 0:cols].rearrange("q (n p) -> q n p", p=32),
                                E2.ap()[:, tpar, 0:cols].rearrange("q (n p) -> q n p", p=32),
                                gs12.ap()[:, tpar, 0:ng, None].broadcast_to([128, ng, 32]),
                                op=ALU.mult,
                                waits=[self.vsem(), *self.use(f"D2{tt % 4}")],
                                incs=[(sV, 1)])
                        self.prod(f"E2{tpar}", sV)
                        wds[tt] = (sV, self.n(sV))
                    emit_xacc(14, wds.pop(14))
                    emit_xacc(15, wds.pop(15))
                    w_acc = (sT, self.n(sT))
                    for j in range(gsz):
                        for hd in range(2):
                            bank = 2 + j * 2 + hd
                            self.em("v", "tensor_copy",
                                    X_half.ap()[:, b0g + j, hd * 512:(hd + 1) * 512],
                                    pb[bank].ap()[:, :],
                                    waits=[w_acc, *self.use("XhW")], incs=[(sV, 1)])
                            self.prod(f"pb{bank}", sV)
                    self.prod(f"phi{gpar}", sT)
                    gi_all += 1
                w_ev = (sV, self.n(sV))
                if H == 0 and "xhalf" in self.taps:
                    self.dma("y", dTap, taps["d_xh"][:, :, :], X_half.ap()[:, :, :],
                             res_waits=[w_ev])
                xd = self.dma("g", dXh, rs_in[H].rearrange("(blk p) c -> p blk c", p=128),
                              X_half.ap()[:, :, :], res_waits=[w_ev])
                self.guard["XhW"] = [(dXh, xd)]
                if self.upto != "p1nors":
                    self.em("g", "collective_compute", "ReduceScatter", ALU.add,
                            replica_groups=[list(range(8))],
                            ins=[rs_in[H]], outs=[rs_out[H]],
                            waits=[(dXh, xd)], incs=[(ccRS[H], 1)])

            if "rs" in self.taps:
                for H in range(2):
                    self.em("y", "wait_ge", ccRS[H], 1)
                    self.dma("y", dTap, taps["d_rs"][H], rs_out[H])

            if self.upto in ("p1", "p1nors"):
                if self.upto == "p1":
                    self.em("g", "wait_ge", ccRS[0], 1)
                    self.em("g", "wait_ge", ccRS[1], 1)
                self.em("v", "engine_nop", incs=[(sV, 1)])
                self.dma("y", dOu[0], d["o"][0:128, :], x_attn.ap()[:, 0, 0:1024],
                         res_waits=[(sV, self.n(sV))])
                self.em("y", "wait_ge", dOu[0], self.n(dOu[0]))
                if self.n(dTap):
                    self.em("y", "wait_ge", dTap, self.n(dTap))
                self.finish(block)
                return d, taps

            self.join_all()
            self.em("v", "memset", S_bd.ap()[:, :, :], 0.0, incs=[(sV, 1)])

            def slot_ap(i, sl):
                if i >= 4 and sl == 0:
                    return wring5
                return wring.ap()[:, (i * 8 + sl) % 4]

            def load_slot(i, sl):
                if i >= 4 and sl == 0:
                    r = 4
                    dst = wring5
                else:
                    r = (i * 8 + sl) % 4
                    dst = wring.ap()[:, r]
                sem = dW[r % 4] if r < 4 else dEw[0]
                ld = self.dma("g", sem, dst,
                              d["WA"][i, sl].rearrange("(cc p) f -> p cc f", p=128),
                              res_waits=self.use(f"ring{r}"))
                self.guard[f"ring{r}_ld"] = [(sem, ld)]

            def load_bkv(i):
                ld = self.dma("g", dBk, bkv_bf.ap()[0:1, :],
                              d["bqkv"][i:i + 1, 1024:3072],
                              res_waits=self.use("bkvU"))
                self.guard["bkv_ld"] = [(dBk, ld)]

            def load_bqp(i):
                self.dma("g", dBq, bqp_bf.ap()[0:1, 0:1024], d["bqkv"][i:i + 1, 0:1024],
                         res_waits=self.use("bqpU"))
                l2 = self.dma("g", dBq, bqp_bf.ap()[0:1, 1024:2048], d["bproj"][i:i + 1, :])
                self.guard["bqp_ld"] = [(dBq, l2)]

            if self.n_layers > 0:
                if "ring0_ld" not in self.guard:
                    load_slot(0, 0); load_slot(0, 1); load_slot(0, 2)
                load_slot(0, 3)
                load_bkv(0); load_bqp(0)

            exp_step = [0]

            def emit_expert_chunk(step):
                if step < 4:
                    s = step // 2
                    part = step % 2
                    if part == 0:
                        ld = self.dma("g", dXt, xt.ap()[:, :], rs_out[s],
                                      res_waits=[(ccRS[s], 1), *self.use("xt")])
                        self.em("v", "tensor_scalar", xt.ap()[:, :], xt.ap()[:, :],
                                -33000.0, 65000.0, op0=ALU.max, op1=ALU.min,
                                waits=[(dXt, ld)], incs=[(sV, 1)])
                        w_clip = (sV, self.n(sV))
                        w_st = ln_stats(xt.ap()[:, :], w_clip, exsq, "exsq")
                        self.em("s", "activation", xtb.ap()[:, :], xt.ap()[:, :], AF.Tanh,
                                bias=sm.ap()[:, 8:9], scale=sm.ap()[:, 7:8],
                                waits=[w_st], incs=[(sS, 1)])
                        self.prod("sm", sS)
                        self.prod("xt", sS)
                        self.guard["extanh"] = [(sS, self.n(sS))]
                        return
                    w_tanh = self.guard["extanh"][0]
                    for r in range(4):
                        for b in range(2):
                            j = 2 * r + b
                            bank = 6 + b
                            bfv = pb[bank].ap()[:, :].bitcast(BF16)
                            self.em("t", "transpose", bfv[:, 0:128],
                                    xtb.ap()[:, j * 128:(j + 1) * 128], idbf.ap(),
                                    waits=[w_tanh, *self.use(f"pb{bank}")],
                                    incs=[(sT, 1)])
                            self.em("v", "tensor_copy", Xh.ap()[:, s, j, :],
                                    bfv[:, 0:128],
                                    waits=[(sT, self.n(sT))], incs=[(sV, 1)])
                            self.prod(f"pb{bank}", sV)
                    for tl in range(4):
                        for fc in range(2):
                            self.guard[f"kv0_{tl}_{fc}"] = [(sV, self.n(sV))]
                    self.prod(f"XhOK{s}", sV)
                else:
                    j = step - 4
                    s, el = j // 4, j % 4
                    bld = self.dma("g", dEb, eb_bf.ap()[0], d["EB"][j:j + 1, :],
                                   res_waits=self.use("eb"))
                    for qq in range(4):
                        qpar = (j * 4 + qq) % 2
                        wld = self.dma(
                            "g", dEw[qpar], EWq.ap()[:, qpar],
                            d["EW"][j].rearrange("(cc p) f -> p cc f", p=128)[:, :, qq * 256:(qq + 1) * 256],
                            res_waits=self.use(f"EWq{qpar}"))
                        bank = 6 + (j * 4 + qq) % 2
                        for cc in range(8):
                            self.em("t", "matmul", pb[bank].ap()[0:32, 0:256],
                                    Xh.ap()[:, s, cc, el * 32:(el + 1) * 32],
                                    EWq.ap()[:, qpar, cc, :],
                                    start=(cc == 0), stop=False,
                                    waits=([(dEw[qpar], wld), *self.use(f"XhOK{s}"),
                                            *self.use(f"pb{bank}")] if cc == 0 else ()))
                        self.em("t", "matmul", pb[bank].ap()[0:32, 0:256],
                                ones.ap()[0:1, 0:32],
                                eb_bf.ap()[0, :, qq * 256:(qq + 1) * 256],
                                start=False, stop=True,
                                waits=[(dEb, bld)], incs=[(sT, 1)])
                        self.guard[f"EWq{qpar}"] = [(sT, self.n(sT))]
                        self.guard["eb"] = [(sT, self.n(sT))]
                        self.em("v", "tensor_copy",
                                yt_sl.ap()[el * 32:(el + 1) * 32, s, qq * 256:(qq + 1) * 256],
                                pb[bank].ap()[0:32, 0:256],
                                waits=[(sT, self.n(sT))], incs=[(sV, 1)])
                        self.prod(f"pb{bank}", sV)
                if step == 11:
                    self.guard["ring4"] = [(sT, self.n(sT)), (sV, self.n(sV))]
                    w_yt = (sV, self.n(sV))
                    if "yt" in self.taps:
                        self.dma("y", dTap, taps["d_yt"][:, :, :], yt_sl.ap()[:, :, :],
                                 res_waits=[w_yt])
                    ad = self.dma("g", dAg, ag_in.rearrange("(s p) c -> p s c", p=128),
                                  yt_sl.ap()[:, :, :], res_waits=[w_yt])
                    self.em("g", "collective_compute", "AllGather", ALU.bypass,
                            replica_groups=[list(range(8))],
                            ins=[ag_in], outs=[ag_out],
                            waits=[(dAg, ad)], incs=[(ccAG, 1)])

            def inject_expert():
                if exp_step[0] < 12:
                    emit_expert_chunk(exp_step[0])
                    exp_step[0] += 1

            for i in range(self.n_layers):
                self.em("t", "matmul", pb[6].ap()[:, :],
                        zer.ap()[0:1, :], ones.ap()[0:1, 0:512],
                        start=True, stop=False,
                        waits=[cst_dn, consts_dn, *self.use("pb6")])
                for H in range(2):
                    kap = slot_ap(i, 2 * H)
                    vap = slot_ap(i, 2 * H + 1)
                    kslot = 4 if (i >= 4 and H == 0) else (i * 8 + 2 * H) % 4
                    vslot = (i * 8 + 2 * H + 1) % 4
                    if H == 0:
                        if i > 0:
                            load_slot(i, 2); load_slot(i, 3)
                    else:
                        load_slot(i, 4); load_slot(i, 5)
                        if i >= 3 and i + 1 < self.n_layers:
                            load_slot(i + 1, 0)
                    brep = y_buf.ap()[:, 0:4, :].bitcast(F32).rearrange("q a b -> q (a b)")
                    for fc in range(2):
                        boff = 1024 * fc + 512 * H
                        self.em("t", "matmul", pb[4 + fc].ap()[:, :],
                                ones.ap()[0:1, 0:128],
                                bkv_bf.ap()[0:1, boff:boff + 512],
                                start=True, stop=True,
                                waits=[consts_dn, *self.use("bkv_ld"),
                                       *self.use(f"pb{4 + fc}")],
                                incs=[(sT, 1)])
                    w_brepmm = (sT, self.n(sT))
                    for fc in range(2):
                        self.em("v", "tensor_copy", brep[:, fc * 512:(fc + 1) * 512],
                                pb[4 + fc].ap()[:, :],
                                waits=[w_brepmm,
                                       *(self.use("brepW") if fc == 0 else ())],
                                incs=[(sV, 1)])
                        self.prod(f"pb{4 + fc}", sV)
                    w_brep = (sV, self.n(sV))

                    def emit_S(tl, spar, H=H):
                        w_kv = self.use(f"kvE{spar}_{tl}")
                        for hl in range(8):
                            h = 8 * H + hl
                            self.em("t", "matmul",
                                    pb[6].ap()[64 * (h % 2):64 * (h % 2) + 64,
                                               64 * (h // 2):64 * (h // 2) + 64],
                                    kv_buf.ap()[:, spar, tl, 64 * hl:64 * hl + 64],
                                    kv_buf.ap()[:, spar, tl, 512 + 64 * hl:512 + 64 * hl + 64],
                                    start=False, stop=False,
                                    waits=(w_kv if hl == 0 else ()),
                                    incs=([(sT, 1)] if hl == 7 else ()))
                        self.guard[f"kv{spar}_{tl}_0"] = [(sT, self.n(sT))]
                        self.guard[f"kv{spar}_{tl}_1"] = [(sT, self.n(sT))]

                    for tau in range(4):
                        spar = tau % 2
                        self.em("s", "copy",
                                staging.ap()[:, spar],
                                x_attn.ap()[:, :, tau * 512:(tau + 1) * 512],
                                waits=[*self.use(f"stg{spar}"), *self.use(f"xa{tau}")],
                                incs=[(sS, 1)])
                        w_stg = (sS, self.n(sS))
                        for tl in range(4):
                            for fc in range(2):
                                bank = (2 * tl + fc) % 4
                                sap = kap if fc == 0 else vap
                                for cc in range(8):
                                    self.em("t", "matmul", pb[bank].ap()[:, :],
                                            staging.ap()[:, spar, cc, tl * 128:(tl + 1) * 128],
                                            sap[:, cc, :],
                                            start=(cc == 0), stop=(cc == 7),
                                            waits=([w_stg,
                                                    *self.use(f"ring{kslot}_ld"),
                                                    *self.use(f"ring{vslot}_ld"),
                                                    *self.use(f"pb{bank}")] if cc == 0 else ()),
                                            incs=([(sT, 1)] if cc == 7 else ()))
                                self.em("v", "tensor_tensor",
                                        kv_buf.ap()[:, spar, tl, fc * 512:(fc + 1) * 512],
                                        pb[bank].ap()[:, :],
                                        brep[:, fc * 512:(fc + 1) * 512],
                                        op=ALU.add,
                                        waits=[(sT, self.n(sT)), w_brep,
                                               *self.use(f"kv{spar}_{tl}_{fc}")],
                                        incs=[(sV, 1)])
                                self.prod(f"pb{bank}", sV)
                            self.guard[f"kvE{spar}_{tl}"] = [(sV, self.n(sV))]
                            if tl >= 1:
                                emit_S(tl - 1, spar)
                        emit_S(3, spar)
                        self.guard[f"stg{spar}"] = [(sT, self.n(sT))]
                    self.guard[f"ring{kslot}"] = [(sT, self.n(sT))]
                    self.guard[f"ring{vslot}"] = [(sT, self.n(sT))]
                    self.guard["brepW"] = [(sV, self.n(sV))]
                self.guard["bkvU"] = [(sT, self.n(sT))]
                if i + 1 < self.n_layers:
                    load_bkv(i + 1)
                self.em("t", "matmul", pb[6].ap()[:, :],
                        zer.ap()[0:1, :], ones.ap()[0:1, 0:512],
                        start=False, stop=True, incs=[(sT, 1)])
                w_Sdone = (sT, self.n(sT))
                self.em("v", "tensor_copy", S_sb.ap()[:, :], pb[6].ap()[:, :],
                        waits=[w_Sdone, *self.use("Ssb")], incs=[(sV, 1)])
                self.prod("pb6", sV)
                sd1 = self.dma("g", dSio, s_in, S_sb.ap()[:, :],
                               res_waits=[(sV, self.n(sV))])
                self.em("g", "collective_compute", "AllReduce", ALU.add,
                        replica_groups=[[0, 1, 2, 3], [4, 5, 6, 7]],
                        ins=[s_in], outs=[s_out],
                        waits=[(dSio, sd1)], incs=[(ccS[i], 1)])
                sd2 = self.dma("g", dSio, S_sb.ap()[:, :], s_out,
                               res_waits=[(ccS[i], 1)])
                w_Sback = (dSio, sd2)
                for hp in range(8):
                    for par in range(2):
                        self.em("v", "tensor_scalar_mul",
                                S_bd.ap()[64 * par:64 * par + 64, hp, 64 * par:64 * par + 64],
                                S_sb.ap()[64 * par:64 * par + 64, 64 * hp:64 * hp + 64],
                                0.125,
                                waits=([w_Sback, *self.use("Sbd")]
                                       if (hp == 0 and par == 0) else ()),
                                incs=([(sV, 1)] if (hp == 7 and par == 1) else ()))
                self.prod("Ssb", sV)
                w_Sbd = (sV, self.n(sV))

                load_slot(i, 6); load_slot(i, 7)
                q0 = (i * 8 + 4) % 4
                q1 = (i * 8 + 5) % 4
                p0 = (i * 8 + 6) % 4
                p1 = (i * 8 + 7) % 4
                for j in range(16):
                    bcoff = 128 * j if j < 8 else 1024 + 128 * (j - 8)
                    self.em("t", "matmul", pb[7].ap()[:, j:j + 1],
                            bqp_bf.ap()[0:1, bcoff:bcoff + 128],
                            ones.ap()[0:1, 0:1],
                            start=True, stop=True,
                            waits=([consts_dn, *self.use("bqp_ld"), *self.use("pb7")]
                                   if j == 0 else ()),
                            incs=([(sT, 1)] if j == 15 else ()))
                self.em("v", "tensor_copy", sm.ap()[:, 32:48], pb[7].ap()[:, 0:16],
                        waits=[(sT, self.n(sT)), *self.use("bcol")], incs=[(sV, 1)])
                self.prod("pb7", sV)
                w_bcol = (sV, self.n(sV))
                for tau in range(4):
                    spar = tau % 2
                    self.em("s", "copy",
                            staging.ap()[:, spar],
                            x_attn.ap()[:, :, tau * 512:(tau + 1) * 512],
                            waits=self.use(f"stg{spar}"), incs=[(sS, 1)])
                    w_stg = (sS, self.n(sS))
                    for fq in range(8):
                        bank = fq % 2
                        slot = q0 if fq < 4 else q1
                        for cc in range(8):
                            self.em("t", "matmul", pb[bank].ap()[:, :],
                                    wring.ap()[:, slot, cc, (fq % 4) * 128:(fq % 4 + 1) * 128],
                                    staging.ap()[:, spar, cc, :],
                                    start=(cc == 0), stop=(cc == 7),
                                    waits=([w_stg, *self.use(f"ring{q0}_ld"),
                                            *self.use(f"ring{q1}_ld"),
                                            *self.use(f"pb{bank}")] if cc == 0 else ()),
                                    incs=([(sT, 1)] if cc == 7 else ()))
                        self.em("s", "activation", q_buf.ap()[:, fq, :],
                                pb[bank].ap()[:, :], AF.Identity,
                                bias=sm.ap()[:, 32 + fq:33 + fq],
                                waits=[(sT, self.n(sT)), w_bcol, *self.use(f"qb{fq}")],
                                incs=[(sS, 1)])
                        self.prod(f"pb{bank}", sS)
                    w_q = (sS, self.n(sS))
                    self.guard[f"stg{spar}"] = [(sT, self.n(sT))]
                    if tau == 3:
                        self.guard[f"ring{q0}"] = [(sT, self.n(sT))]
                        self.guard[f"ring{q1}"] = [(sT, self.n(sT))]
                        if i + 1 < self.n_layers:
                            if i + 1 < 4:
                                load_slot(i + 1, 0)
                            load_slot(i + 1, 1)
                    for hp in range(8):
                        bank = 2 + hp % 2
                        self.em("t", "matmul", pb[bank].ap()[:, :],
                                S_bd.ap()[:, hp, :], q_buf.ap()[:, hp, :],
                                start=True, stop=True,
                                waits=[w_Sbd, w_q, *self.use(f"pb{bank}")],
                                incs=[(sT, 1)])
                        self.guard[f"qb{hp}"] = [(sT, self.n(sT))]
                        self.em("v", "tensor_copy", y_buf.ap()[:, hp, :],
                                pb[bank].ap()[:, :],
                                waits=[(sT, self.n(sT)), *self.use(f"yb{hp}"),
                                       *(self.use("brepW") if (tau == 0 and hp == 0) else ())],
                                incs=[(sV, 1)])
                        self.prod(f"pb{bank}", sV)
                    w_y = (sV, self.n(sV))
                    for co in range(8):
                        bank = 4 + co % 2
                        slot = p0 if co < 4 else p1
                        for cc in range(8):
                            self.em("t", "matmul", pb[bank].ap()[:, :],
                                    wring.ap()[:, slot, cc, (co % 4) * 128:(co % 4 + 1) * 128],
                                    y_buf.ap()[:, cc, :],
                                    start=(cc == 0), stop=(cc == 7),
                                    waits=([w_y, *self.use(f"ring{p0}_ld"),
                                            *self.use(f"ring{p1}_ld"),
                                            *self.use(f"pb{bank}")] if cc == 0 else ()),
                                    incs=([(sT, 1)] if cc == 7 else ()))
                        if co == 7:
                            for hp in range(8):
                                self.guard[f"yb{hp}"] = [(sT, self.n(sT))]
                        self.em("v", "scalar_tensor_tensor",
                                x_attn.ap()[:, co, tau * 512:(tau + 1) * 512],
                                pb[bank].ap()[:, :],
                                sm.ap()[:, 40 + co:41 + co],
                                x_attn.ap()[:, co, tau * 512:(tau + 1) * 512],
                                op0=ALU.add, op1=ALU.add,
                                waits=[(sT, self.n(sT)), w_bcol], incs=[(sV, 1)])
                        self.prod(f"pb{bank}", sV)
                    self.guard[f"xa{tau}"] = [(sV, self.n(sV))]
                    if i < 3:
                        inject_expert()
                self.guard["Sbd"] = [(sT, self.n(sT))]
                self.guard["bcol"] = [(sV, self.n(sV))]
                self.guard[f"ring{p0}"] = [(sT, self.n(sT))]
                self.guard[f"ring{p1}"] = [(sT, self.n(sT))]
                self.guard["bqpU"] = [(sT, self.n(sT))]
                if i + 1 < self.n_layers:
                    load_bqp(i + 1)

                if "xa" in self.taps and i == self.n_layers - 1:
                    self.dma("y", dTap, taps["d_xa"][:, :, :], x_attn.ap()[:, :, :],
                             res_waits=[(sV, self.n(sV))])
                if "S" in self.taps and i == 0:
                    self.dma("y", dTap, taps["d_S"][:, :], S_sb.ap()[:, :],
                             res_waits=[(sV, self.n(sV))])

            while exp_step[0] < 12:
                inject_expert()

            if not self.endgame:
                self.em("v", "engine_nop", incs=[(sV, 1)])
                self.dma("y", dOu[0], d["o"][0:128, :], x_attn.ap()[:, 0, 0:1024],
                         res_waits=[(sV, self.n(sV))])
                self.em("y", "wait_ge", dOu[0], self.n(dOu[0]))
                if self.n(dTap):
                    self.em("y", "wait_ge", dTap, self.n(dTap))
                self.finish(block)
                return d, taps

            self.join_all()

            pfl = self.dma("g", dPf, phi_full.ap()[:, :, :],
                           d["phiT"].rearrange("(cc p) e -> p cc e", p=128))
            w_pf = (dPf, pfl)

            def emit_ct_transposes(tg, tloc):
                cpar = (4 * tg + tloc) % 3
                w_C = self.use(f"CbufR{cpar}")
                for k in range(16):
                    bank = 4 + k % 4
                    bfv = pb[bank].ap()[:, :].bitcast(BF16)
                    self.em("t", "transpose", bfv[:, 0:128],
                            Cbuf.ap()[:, cpar, k * 128:(k + 1) * 128], idbf.ap(),
                            waits=[*w_C, *self.use(f"pbt{bank}"),
                                   *(self.use(f"pb{bank}") if k < 4 else ())],
                            incs=[(sT, 1)])
                    self.em("v", "tensor_copy",
                            Ct.ap()[:, k, tloc * 128:(tloc + 1) * 128],
                            bfv[:, 0:128],
                            waits=[(sT, self.n(sT)),
                                   *(self.use("Ct") if k == 0 else ())],
                            incs=[(sV, 1)])
                    self.guard[f"pbt{bank}"] = [(sV, self.n(sV))]
                self.guard[f"Cbuf{cpar}"] = [(sT, self.n(sT))]

            for tg in range(4):
                for tloc in range(4):
                    tt = 4 * tg + tloc
                    tpar = tt % 2
                    cpar = tt % 3
                    xld = self.dma("y", dXn[tpar], xntau.ap()[:, tpar],
                                   xn_spill[:, :, tt * 128:(tt + 1) * 128],
                                   res_waits=self.use(f"xnt{tpar}"))
                    w_xn = (dXn[tpar], xld)
                    for ch in range(4):
                        for cc in range(8):
                            self.em("t", "matmul", pb[ch].ap()[:, :],
                                    xntau.ap()[:, tpar, cc, :],
                                    phi_full.ap()[:, cc, ch * 512:(ch + 1) * 512],
                                    start=(cc == 0), stop=False,
                                    waits=([w_pf, w_xn, *self.use(f"pb{ch}")]
                                           if cc == 0 else ()))
                        self.em("t", "matmul", pb[ch].ap()[:, :],
                                ones.ap()[0:1, 0:128],
                                phib_bf.ap()[0:1, ch * 512:(ch + 1) * 512],
                                start=False, stop=True,
                                incs=([(sT, 1)] if ch == 3 else ()))
                    self.guard[f"xnt{tpar}"] = [(sT, self.n(sT))]
                    w_log = (sT, self.n(sT))
                    for ch in range(4):
                        self.em("v", "reduce_max", sm.ap()[:, 16 + ch:17 + ch],
                                pb[ch].ap()[:, :], axis=AX.X, negate=True,
                                waits=([w_log, *self.use("smE")] if ch == 0 else ()),
                                incs=[(sV, 1)])
                    self.em("v", "tensor_reduce", sm.ap()[:, 20:21],
                            sm.ap()[:, 16:20], axis=AX.X, op=ALU.min,
                            waits=[self.vsem()], incs=[(sV, 1)])
                    w_max = (sV, self.n(sV))
                    for ch in range(4):
                        self.em("s", "activation", Ebuf.ap()[:, ch * 512:(ch + 1) * 512],
                                pb[ch].ap()[:, :], AF.Exp, bias=sm.ap()[:, 20:21],
                                waits=([w_max, *self.use("Ebuf")] if ch == 0 else ()),
                                incs=[(sS, 1)])
                        self.prod(f"pb{ch}", sS)
                    self.prod("smE", sS)
                    w_exp = (sS, self.n(sS))
                    self.em("v", "reduce_sum",
                            sm.ap()[:, 24:56].rearrange("q (p o) -> q p o", o=1),
                            Ebuf.ap()[:, :].rearrange("q (n p) -> q p n", p=32),
                            axis=AX.X, waits=[w_exp], incs=[(sV, 1)])
                    self.em("v", "reciprocal", sm.ap()[:, 24:56], sm.ap()[:, 24:56],
                            waits=[self.vsem()], incs=[(sV, 1)])
                    self.em("v", "tensor_tensor",
                            Cbuf.ap()[:, cpar, :].rearrange("q (n p) -> q n p", p=32),
                            Ebuf.ap()[:, :].rearrange("q (n p) -> q n p", p=32),
                            sm.ap()[:, 24:56][:, None, :].broadcast_to([128, 64, 32]),
                            op=ALU.mult,
                            waits=[self.vsem(), *self.use(f"Cbuf{cpar}")],
                            incs=[(sV, 1)])
                    self.prod("Ebuf", sV)
                    self.guard[f"CbufR{cpar}"] = [(sV, self.n(sV))]
                    if tloc >= 2:
                        emit_ct_transposes(tg, tloc - 2)
                emit_ct_transposes(tg, 2)
                emit_ct_transposes(tg, 3)
                w_ct_all = (sV, self.n(sV))
                for npc in range(16):
                    ypar = npc % 2
                    row0 = 256 * npc if npc < 8 else 256 * (npc - 8) + 128
                    yld = self.dma("y", dYt[ypar], yt2.ap()[:, ypar],
                                   ag_out[row0:row0 + 128, :],
                                   res_waits=[(ccAG, 1), *self.use(f"yt2{ypar}")])
                    w_yt = (dYt[ypar], yld)
                    for ch in range(8):
                        self.em("t", "matmul", pb[ch].ap()[:, :],
                                yt2.ap()[:, ypar, ch * 128:(ch + 1) * 128],
                                Ct.ap()[:, npc, :],
                                start=(npc == 0), stop=(npc == 15),
                                waits=([w_yt, w_ct_all, *self.use(f"pb{ch}"),
                                        *self.use("pbt4"), *self.use("pbt5"),
                                        *self.use("pbt6"), *self.use("pbt7")]
                                       if npc == 0 else [w_yt]),
                                incs=([(sT, 1)] if ch == 7 else ()))
                    self.guard[f"yt2{ypar}"] = [(sT, self.n(sT))]
                self.guard["Ct"] = [(sT, self.n(sT))]
                w_Y = (sT, self.n(sT))
                for ch in range(8):
                    self.em("s", "activation", th_s.ap()[:, :], pb[ch].ap()[:, :],
                            AF.Tanh, scale=cst.ap()[:, 1:2],
                            waits=[w_Y, cst_dn, *self.use("th_s")], incs=[(sS, 1)])
                    w_th = (sS, self.n(sS))
                    self.em("v", "scalar_tensor_tensor", gs_buf.ap()[:, :],
                            th_s.ap()[:, :], 1.0, pb[ch].ap()[:, :],
                            op0=ALU.add, op1=ALU.mult,
                            waits=[w_th, *self.use("gs_buf")], incs=[(sV, 1)])
                    self.prod("th_s", sV)
                    self.prod(f"pb{ch}", sV)
                    self.em("v", "scalar_tensor_tensor", sgx.ap()[:, ch, :],
                            gs_buf.ap()[:, :], 0.5,
                            x_attn.ap()[:, ch, tg * 512:(tg + 1) * 512],
                            op0=ALU.mult, op1=ALU.mult,
                            waits=[self.vsem()], incs=[(sV, 1)])
                    self.prod("gs_buf", sV)
                    self.guard[f"sgc{ch}"] = [(sV, self.n(sV))]
                for tloc in range(4):
                    tt = 4 * tg + tloc
                    opar = tt % 2
                    xld = self.dma("y", dXi[opar], xo.ap()[:, opar],
                                   d["x"][tt * 128:(tt + 1) * 128, :],
                                   res_waits=self.use(f"xo{opar}"))
                    w_xo = (dXi[opar], xld)
                    for rnd in range(2):
                        for k in range(4):
                            ch = 4 * rnd + k
                            bank = 4 + k
                            self.em("t", "transpose", pb[bank].ap()[:, 0:128],
                                    sgx.ap()[:, ch, tloc * 128:(tloc + 1) * 128],
                                    idf32.ap(),
                                    waits=[*self.use(f"sgc{ch}"),
                                           *self.use(f"pbe{bank}"),
                                           *(self.use(f"pb{bank}") if (tloc == 0 and rnd == 0) else ())],
                                    incs=[(sT, 1)])
                            self.em("v", "tensor_tensor",
                                    outt.ap()[:, opar, ch * 128:(ch + 1) * 128],
                                    pb[bank].ap()[:, 0:128],
                                    xo.ap()[:, opar, ch * 128:(ch + 1) * 128],
                                    op=ALU.add,
                                    waits=[(sT, self.n(sT)), w_xo,
                                           *(self.use(f"outt{opar}") if ch == 0 else ())],
                                    incs=[(sV, 1)])
                            self.guard[f"pbe{bank}"] = [(sV, self.n(sV))]
                    st = self.dma("y", dOu[opar], d["o"][tt * 128:(tt + 1) * 128, :],
                                  outt.ap()[:, opar], res_waits=[(sV, self.n(sV))])
                    self.guard[f"outt{opar}"] = [(dOu[opar], st)]
                    self.guard[f"xo{opar}"] = [(sV, self.n(sV))]
                for ch in range(4):
                    self.guard[f"pb{4 + ch}"] = self.guard[f"pbe{4 + ch}"]

            self.em("y", "wait_ge", dOu[0], self.n(dOu[0]))
            self.em("y", "wait_ge", dOu[1], self.n(dOu[1]))
            if self.n(dTap):
                self.em("y", "wait_ge", dTap, self.n(dTap))
            self.finish(block)
        return d, taps

    def finish(self, block):
        streams = self.streams

        @block.tensor
        def _(e):
            for fn in streams["t"]:
                fn(e)

        @block.vector
        def _(e):
            for fn in streams["v"]:
                fn(e)

        @block.scalar
        def _(e):
            for fn in streams["s"]:
                fn(e)

        @block.gpsimd
        def _(e):
            for fn in streams["g"]:
                fn(e)

        @block.sync
        def _(e):
            for fn in streams["y"]:
                fn(e)



def _pack_inputs(L0_attn_inputs_embeds, L0_attn_Wqkv, L0_attn_bqkv, L0_attn_Wproj,
                 L0_attn_bproj, L0_ffn_phi_w, L0_ffn_phi_b, L0_ffn_expert_w,
                 L0_ffn_expert_b):
    x = np.ascontiguousarray(np.asarray(L0_attn_inputs_embeds, dtype=np.float32)
                             .reshape(2 * 8192, C))
    Wqkv = np.asarray(L0_attn_Wqkv, dtype=np.float32)
    Wproj = np.asarray(L0_attn_Wproj, dtype=np.float32)
    bqkv = np.ascontiguousarray(np.asarray(L0_attn_bqkv, dtype=np.float32))
    bproj = np.ascontiguousarray(np.asarray(L0_attn_bproj, dtype=np.float32))
    phiT = np.ascontiguousarray(np.asarray(L0_ffn_phi_w, dtype=np.float32).T)
    phib = np.ascontiguousarray(np.asarray(L0_ffn_phi_b, dtype=np.float32).reshape(1, NP))
    EWall = np.asarray(L0_ffn_expert_w, dtype=np.float32)
    EBall = np.asarray(L0_ffn_expert_b, dtype=np.float32)

    WA = np.empty((NLAYER, 8, C, 512), dtype=np.float32)
    for l in range(NLAYER):
        W1 = Wqkv[l].T
        WP = Wproj[l].T
        WA[l, 0] = W1[:, 1024:1536]
        WA[l, 1] = W1[:, 2048:2560]
        WA[l, 2] = W1[:, 1536:2048]
        WA[l, 3] = W1[:, 2560:3072]
        WA[l, 4] = W1[:, 0:512]
        WA[l, 5] = W1[:, 512:1024]
        WA[l, 6] = WP[:, 0:512]
        WA[l, 7] = WP[:, 512:1024]
    WA = np.ascontiguousarray(WA)

    ones = np.ones((1, 512), dtype=BF)
    idbf = np.eye(128, dtype=BF)
    idf32 = np.eye(128, dtype=np.float32)

    in_maps = []
    for c in range(NCORES):
        experts = [4 * c + j for j in range(4)] + [32 + 4 * c + j for j in range(4)]
        in_maps.append({
            "x": np.ascontiguousarray(x[c * T:(c + 1) * T]),
            "WA": WA, "bqkv": bqkv, "bproj": bproj,
            "phiT": phiT, "phib": phib,
            "EW": np.ascontiguousarray(EWall[experts]),
            "EB": np.ascontiguousarray(EBall[experts]),
            "ones": ones, "idbf": idbf, "idf32": idf32,
        })
    return in_maps


_CACHE = {}


def _get_builder():
    if "kb" not in _CACHE:
        kb = KB()
        kb.build()
        _CACHE["kb"] = kb
    return _CACHE["kb"]


def kernel(**inputs):
    kb = _get_builder()
    in_maps = _pack_inputs(**inputs)
    res = run_bass_kernel_spmd(kb.nc, in_maps, core_ids=list(range(NCORES)))
    out = np.concatenate([np.asarray(r["o"]) for r in res.results], axis=0)
    return np.ascontiguousarray(out.reshape(2, 8192, C).astype(np.float32, copy=False))
